# version 21
# baseline (speedup 1.0000x reference)
import contextlib
import numpy as np
import ml_dtypes
import concourse.bass as bass
import concourse.mybir as mybir
from concourse.bass_utils import run_bass_kernel_spmd

F32 = mybir.dt.float32
BF16 = mybir.dt.bfloat16
AF = mybir.ActivationFunctionType
ALU = mybir.AluOpType
AX = mybir.AxisListType

NCORES = 8
SEQ = 4
T = 2048
D = 1024
H = 16
DH = 64
W = 512
NBLK = 4
DFF = 4096
NE = 8
RK = 16
TT = 256
NFC = DFF // 128
NTILE = SEQ * T // TT
EXP_BIAS = -12.0

ENGS = ("sync", "gpsimd", "tensor", "vector", "scalar")


class Sem:
    def __init__(self, handle, step, eng):
        self.h = handle
        self.step = step
        self.eng = eng
        self.n = 0

    def tok(self):
        self.n += self.step
        return (self, self.n)


class Emitter:
    def __init__(self):
        self.ops = {e: [] for e in ENGS}
        self.satisfied = {e: {} for e in ENGS}

    def emit(self, eng, fn, waits=(), inc=None, dr=False, war=False):
        need = []
        sat = self.satisfied[eng]
        if inc is not None and inc.step == 16:
            waits = list(waits) + [(inc, inc.n)]
        for wt in waits:
            if wt is None:
                continue
            sem, val = wt
            if sem.eng == eng and sem.step == 1:
                continue
            if sat.get(id(sem), 0) >= val:
                continue
            need.append((sem, val))
            sat[id(sem)] = val
        tok = inc.tok() if inc is not None else None
        step = inc.step if inc is not None else 0
        drain = (inc is not None and inc.step == 1 and not war
                 and eng in ("vector", "scalar"))

        def run(e, _need=tuple(need), _tok=tok, _fn=fn, _step=step, _dr=drain, _pre=dr):
            for (sem, val) in _need:
                e.wait_ge(sem.h, val)
            if _pre:
                e.drain()
            ins = _fn(e)
            if _tok is not None:
                if _dr:
                    e.drain().then_inc(_tok[0].h, _step)
                else:
                    ins.then_inc(_tok[0].h, _step)

        self.ops[eng].append(run)
        return tok


class Buf:
    def __init__(self):
        self.ready = []
        self.readers = []

    def wr_waits(self):
        return list(self.ready) + list(self.readers)

    def set_ready(self, tok):
        self.ready = [tok] if tok is not None else []
        self.readers = []

    def add_reader(self, tok):
        if tok is not None:
            self.readers.append(tok)


def dedup(waits):
    best = {}
    for w in waits:
        if w is None:
            continue
        sem, val = w
        k = id(sem)
        if k not in best or best[k][1] < val:
            best[k] = (sem, val)
    return list(best.values())


def sc_ranges(n):
    rr = []
    if n > 0:
        for jc in range(4):
            rr.append(("prev", jc, 0, min(W, 128 * (jc + 1))))
    for jc in range(4):
        rr.append(("own", jc, 128 * jc, W - 128 * jc))
    return rr


def build_program(dbg=False):
    nc = bass.Bass("TRN2", target_bir_lowering=False)
    scr_kind = dict(kind="ExternalOutput") if dbg else {}

    x_in = nc.dram_tensor("x", [SEQ * T, D], F32, kind="ExternalInput").ap()
    wqT = nc.dram_tensor("wqT", [D, D], F32, kind="ExternalInput").ap()
    wkT = nc.dram_tensor("wkT", [D, D], F32, kind="ExternalInput").ap()
    wvT = nc.dram_tensor("wvT", [D, D], F32, kind="ExternalInput").ap()
    woT = nc.dram_tensor("woT", [D, D], F32, kind="ExternalInput").ap()
    wiT_in = nc.dram_tensor("wiT", [D, DFF], F32, kind="ExternalInput").ap()
    wioT_in = nc.dram_tensor("wioT", [DFF, D], F32, kind="ExternalInput").ap()
    bi2_in = nc.dram_tensor("bi2", [128, NFC], F32, kind="ExternalInput").ap()
    bo_in = nc.dram_tensor("bo", [1, D], F32, kind="ExternalInput").ap()
    aT_in = nc.dram_tensor("aT", [D, NE * RK], F32, kind="ExternalInput").ap()
    bst_in = nc.dram_tensor("bst", [NE * RK, DFF], F32, kind="ExternalInput").ap()
    gwT_in = nc.dram_tensor("gwT", [D, NE], F32, kind="ExternalInput").ap()
    gb_in = nc.dram_tensor("gb", [1, NE], F32, kind="ExternalInput").ap()
    ident_in = nc.dram_tensor("ident", [128, 128], BF16, kind="ExternalInput").ap()
    maskA_in = nc.dram_tensor("maskA", [128, 128], BF16, kind="ExternalInput").ap()
    maskB_in = nc.dram_tensor("maskB", [128, 128], BF16, kind="ExternalInput").ap()
    ones_in = nc.dram_tensor("ones_row", [1, 512], BF16, kind="ExternalInput").ap()
    rep_in = nc.dram_tensor("rep", [NE, 128], BF16, kind="ExternalInput").ap()

    out_hbm = nc.dram_tensor("out", [SEQ * T, D], F32, kind="ExternalOutput").ap()

    if dbg:
        dbgx = nc.dram_tensor("dbgx", [128, 4 * D], F32, kind="ExternalOutput").ap()
        dbgxn = nc.dram_tensor("dbgxn", [128, 4 * D], BF16, kind="ExternalOutput").ap()
    xnT_scr = nc.dram_tensor("xnT_scr", [SEQ, 8, 128, T], BF16, **scr_kind).ap()
    s1_scr = nc.dram_tensor("s1_scr", [128, SEQ * T], BF16, **scr_kind).ap()
    s2_scr = nc.dram_tensor("s2_scr", [128, SEQ * T], BF16, **scr_kind).ap()
    attnp_scr = nc.dram_tensor("attnp_scr", [SEQ * T, D], F32, **scr_kind).ap()

    em = Emitter()
    stack = contextlib.ExitStack()
    with stack:
        sQL = Sem(stack.enter_context(nc.semaphore("sQL")), 16, "sync")
        sQC = Sem(stack.enter_context(nc.semaphore("sQC")), 16, "sync")
        sQC2 = Sem(stack.enter_context(nc.semaphore("sQC2")), 16, "gpsimd")
        sQG = Sem(stack.enter_context(nc.semaphore("sQG")), 16, "gpsimd")
        sQW = Sem(stack.enter_context(nc.semaphore("sQW")), 16, "gpsimd")
        sQS = Sem(stack.enter_context(nc.semaphore("sQS")), 16, "scalar")
        sPE = Sem(stack.enter_context(nc.semaphore("sPE")), 1, "tensor")
        sGP = Sem(stack.enter_context(nc.semaphore("sGP")), 1, "gpsimd")
        sVE = Sem(stack.enter_context(nc.semaphore("sVE")), 1, "vector")
        sSE = Sem(stack.enter_context(nc.semaphore("sSE")), 1, "scalar")

        ident = stack.enter_context(nc.sbuf_tensor("ident_sb", [128, 128], BF16))
        maskA = stack.enter_context(nc.sbuf_tensor("maskA_sb", [128, 128], BF16))
        maskB = stack.enter_context(nc.sbuf_tensor("maskB_sb", [128, 128], BF16))
        ones_row = stack.enter_context(nc.sbuf_tensor("ones_sb", [1, 512], BF16))
        rep_sb = stack.enter_context(nc.sbuf_tensor("rep_sb", [NE, 128], BF16))
        vs_all = stack.enter_context(nc.sbuf_tensor("vs_all", [128, SEQ * 16], F32))
        fc_sb = stack.enter_context(nc.sbuf_tensor("fc_sb", [128, 2], F32))
        gb_sb = stack.enter_context(nc.sbuf_tensor("gb_sb", [1, NE], BF16))
        bi2_sb = stack.enter_context(nc.sbuf_tensor("bi2_sb", [128, NFC], F32))
        bo_f = stack.enter_context(nc.sbuf_tensor("bo_f", [1, D], F32))
        bo2_sb = stack.enter_context(nc.sbuf_tensor("bo2_sb", [1, D], BF16))

        for dst, srcap in ((ident, ident_in), (maskA, maskA_in), (maskB, maskB_in),
                           (ones_row, ones_in), (rep_sb, rep_in)):
            em.emit("sync", lambda e, d=dst, s=srcap: e.dma_start(out=d[:, :], in_=s[:, :]), inc=sQC)
        em.emit("gpsimd", lambda e: e.dma_start(out=gb_sb[:, :], in_=gb_in[:, :]), inc=sQC2)
        em.emit("sync", lambda e: e.dma_start(out=bi2_sb[:, :], in_=bi2_in[:, :]), inc=sQC)
        em.emit("sync", lambda e: e.dma_start(out=bo_f[:, :], in_=bo_in[:, :]), inc=sQC)
        consts_tok = (sQC, sQC.n)
        consts_tok2 = (sQC2, sQC2.n)
        t_bo2 = em.emit("scalar", lambda e: e.activation(
            bo2_sb[:, :], bo_f[:, :], AF.Identity, scale=2.0), waits=[consts_tok], inc=sSE)
        em.emit("vector", lambda e: e.memset(fc_sb[:, 0:1], EXP_BIAS))
        t_fc = em.emit("vector", lambda e: e.memset(fc_sb[:, 1:2], 1e-5), inc=sVE)

        pa = contextlib.ExitStack()
        with pa:
            wq_sb = pa.enter_context(nc.sbuf_tensor("wq_sb", [128, 8 * D], BF16))
            wk_sb = pa.enter_context(nc.sbuf_tensor("wk_sb", [128, 8 * D], BF16))
            wv_sb = pa.enter_context(nc.sbuf_tensor("wv_sb", [128, 8 * D], BF16))
            wo_sb = pa.enter_context(nc.sbuf_tensor("wo_sb", [128, 8 * D], BF16))
            gw_sb = pa.enter_context(nc.sbuf_tensor("gw_sb", [128, 8 * NE], BF16))
            aT_sb = pa.enter_context(nc.sbuf_tensor("aT_sb", [128, 8 * 128], BF16))

            x_blk = pa.enter_context(nc.sbuf_tensor("x_blk", [128, 4 * D], F32))
            x_res = [pa.enter_context(nc.sbuf_tensor(f"x_res{i}", [128, D], F32)) for i in range(2)]
            xn_o = pa.enter_context(nc.sbuf_tensor("xn_o", [128, 4 * D], BF16))
            o_sb = pa.enter_context(nc.sbuf_tensor("o_sb", [128, 4 * D], BF16))
            lnst = pa.enter_context(nc.sbuf_tensor("lnst", [128, 32], F32))
            xnT_blk = pa.enter_context(nc.sbuf_tensor("xnT_blk", [128, 8 * W], BF16))
            qoT_blk2 = [pa.enter_context(nc.sbuf_tensor(f"qoT_blk{i}", [128, 8 * W], BF16)) for i in range(2)]
            kT_blk = [pa.enter_context(nc.sbuf_tensor(f"kT_blk{i}", [128, 8 * W], BF16)) for i in range(2)]
            vaug = [pa.enter_context(nc.sbuf_tensor(f"vaug{i}", [128, 4 * H * 65], BF16)) for i in range(2)]
            expT = [pa.enter_context(nc.sbuf_tensor(f"expT{i}", [128, 8 * W], BF16)) for i in range(3)]
            attnp_ev = [pa.enter_context(nc.sbuf_tensor(f"attnp_ev{i}", [128, D], F32)) for i in range(2)]
            gt = pa.enter_context(nc.sbuf_tensor("gt", [128, 2 * NE], F32))
            gs = pa.enter_context(nc.sbuf_tensor("gs", [128, 8], F32))
            m1m2 = pa.enter_context(nc.sbuf_tensor("m1m2", [128, 4 * 2 * NE], BF16))
            m1T_sb = pa.enter_context(nc.sbuf_tensor("m1T_sb", [NE, W], BF16))
            m2T_sb = pa.enter_context(nc.sbuf_tensor("m2T_sb", [NE, W], BF16))
            s_sb = pa.enter_context(nc.sbuf_tensor("s_sb", [128, W], BF16))
            s12_sb = pa.enter_context(nc.sbuf_tensor("s12_sb", [128, W], BF16))
            rcp = pa.enter_context(nc.sbuf_tensor("rcp", [128, 8], F32))

            ps_ch = [pa.enter_context(nc.psum_tensor(f"ps_ch{i}", [128, 512], F32)) for i in range(2)]
            ps_sc = [pa.enter_context(nc.psum_tensor(f"ps_sc{i}", [128, 512], F32)) for i in range(2)]
            ps_pv = [pa.enter_context(nc.psum_tensor(f"ps_pv{i}", [128, 512], F32)) for i in range(2)]
            ps_tr = [pa.enter_context(nc.psum_tensor(f"ps_tr{i}", [128, 1024], BF16)) for i in range(2)]

            B_x = Buf()
            B_xres = [Buf(), Buf()]
            B_xn = Buf()
            B_o = Buf()
            B_xnT = Buf()
            B_qoT2 = [Buf(), Buf()]
            B_kT = [Buf(), Buf()]
            B_vaug = [Buf(), Buf()]
            B_expT = [Buf(), Buf(), Buf()]
            B_attnp = [Buf(), Buf()]
            B_pch = [Buf(), Buf()]
            B_psc = [Buf(), Buf()]
            B_ppv = [Buf(), Buf()]
            B_ptr = [Buf(), Buf()]
            B_m1m2 = Buf()
            B_m1T = Buf()
            B_m2T = Buf()
            B_s = Buf()
            B_s12 = Buf()
            B_gs = Buf()

            for par in range(2):
                def setones(e, pv=par):
                    v = vaug[pv][:, :].rearrange("p (c x) -> p c x", x=65)
                    return e.memset(v[:, :, 64:65], 1.0)
                B_vaug[par].set_ready(em.emit("vector", setones, inc=sVE))

            wtok = {}
            for nm, dst, src in (("wq", wq_sb, wqT), ("wk", wk_sb, wkT),
                                 ("wv", wv_sb, wvT), ("wo", wo_sb, woT)):
                t = None
                for k4 in range(2):
                    t = em.emit("gpsimd", lambda e, d=dst, s=src, k=k4: e.dma_start(
                        out=d[:, k * 4 * D:(k + 1) * 4 * D].rearrange("p (c d) -> p c d", d=D),
                        in_=s[k * 512:(k + 1) * 512, :].rearrange("(c p) d -> p c d", p=128)), inc=sQG)
                wtok[nm] = t
            for kd in range(8):
                em.emit("gpsimd", lambda e, k=kd: e.dma_start(
                    out=gw_sb[:, k * NE:(k + 1) * NE], in_=gwT_in[k * 128:(k + 1) * 128, :]), inc=sQW)
            t_gw = (sQW, sQW.n)
            t_gb = consts_tok2
            for kd in range(8):
                t_aT = em.emit("gpsimd", lambda e, k=kd: e.dma_start(
                    out=aT_sb[:, k * 128:(k + 1) * 128], in_=aT_in[k * 128:(k + 1) * 128, :]), inc=sQG)
            wq_tok, wk_tok, wv_tok, wo_tok = wtok["wq"], wtok["wk"], wtok["wv"], wtok["wo"]

            state = {"t_xn": None, "tel": None, "m12": [None] * 4, "sblk": None, "proj": None}

            def emit_prologue_a1(s, n):
                bi = s * NBLK + n
                par = bi % 2
                st_t_xn = state["t_xn"]
                st_prev_tel = state["tel"]
                t_x = em.emit("sync", lambda e, ss=s, nn=n: e.dma_start(
                    out=x_blk[:, :].rearrange("p (c d) -> p c d", d=D),
                    in_=x_in[ss * T + nn * W: ss * T + (nn + 1) * W, :]
                        .rearrange("(c p) d -> p c d", p=128)),
                    waits=dedup(B_x.wr_waits()), inc=sQL)
                B_x.set_ready(t_x)

                xn_w = dedup(B_xn.wr_waits())
                for c in range(4):
                    L = 8 * c
                    xc = x_blk[:, c * D:(c + 1) * D]
                    em.emit("vector", lambda e, xc=xc, L=L: e.reduce_sum(
                        lnst[:, L:L + 1], xc, axis=AX.X),
                        waits=dedup([t_x, st_t_xn]))
                    t_sq = em.emit("scalar", lambda e, xc=xc, cc=c, L=L: e.activation(
                        xn_o[:, cc * D:(cc + 1) * D], xc, AF.Square, accum_out=lnst[:, L + 1:L + 2]),
                        waits=dedup([t_x, st_t_xn] + (xn_w if c == 0 else [])), inc=sSE)
                    em.emit("vector", lambda e, L=L: e.tensor_scalar_mul(
                        lnst[:, L + 2:L + 3], lnst[:, L:L + 1], 1.0 / D), dr=True)
                    em.emit("vector", lambda e, L=L: e.tensor_scalar(
                        lnst[:, L + 3:L + 4], lnst[:, L:L + 1], lnst[:, L:L + 1], 1.0 / (D * D),
                        ALU.mult, ALU.mult))
                    t_v = em.emit("vector", lambda e, L=L: e.tensor_scalar(
                        lnst[:, L + 4:L + 5], lnst[:, L + 1:L + 2], 1.0 / D, lnst[:, L + 3:L + 4],
                        ALU.mult, ALU.subtract),
                        waits=[t_sq], inc=sVE, dr=True)
                    em.emit("scalar", lambda e, L=L: e.activation(
                        lnst[:, L + 5:L + 6], lnst[:, L + 4:L + 5], AF.Ln, bias=fc_sb[:, 1:2]),
                        waits=dedup([t_v, t_fc]))
                    t_rs = em.emit("scalar", lambda e, L=L: e.activation(
                        lnst[:, L + 6:L + 7], lnst[:, L + 5:L + 6], AF.Exp, scale=-0.5), inc=sSE, dr=True)
                    t_nm = em.emit("vector", lambda e, L=L: e.tensor_scalar(
                        lnst[:, L + 7:L + 8], lnst[:, L + 2:L + 3], lnst[:, L + 6:L + 7], -1.0,
                        ALU.mult, ALU.mult),
                        waits=[t_rs], inc=sVE)
                    st_t_xn = em.emit("scalar", lambda e, xc=xc, cc=c, L=L: e.activation(
                        xn_o[:, cc * D:(cc + 1) * D], xc, AF.Identity,
                        bias=lnst[:, L + 7:L + 8], scale=lnst[:, L + 6:L + 7]),
                        waits=dedup([t_nm] + xn_w), inc=sSE, dr=True)
                B_xn.set_ready(st_t_xn)
                B_x.add_reader(st_t_xn)
                if dbg and bi == 0:
                    t_d1 = em.emit("scalar", lambda e: e.dma_start(out=dbgx[:, :], in_=x_blk[:, :]),
                                   waits=[st_t_xn], inc=sQS)
                    t_d2 = em.emit("scalar", lambda e: e.dma_start(out=dbgxn[:, :], in_=xn_o[:, :]),
                                   waits=[st_t_xn], inc=sQS)
                    B_x.add_reader(t_d1)
                    B_xn.add_reader(t_d2)

                state["t_xn"] = st_t_xn
                state["tel"] = st_prev_tel

            def emit_prologue_a2(s, n):
                bi = s * NBLK + n
                par = bi % 2
                st_t_xn = state["t_xn"]
                st_prev_tel = state["tel"]
                xnT_w = dedup(B_xnT.wr_waits())
                t_ev = None
                for c in range(4):
                    for kd in range(8):
                        i = (c * 8 + kd) % 2
                        tw = dedup([B_xn.ready[0], consts_tok] + B_ptr[i].wr_waits() +
                                   (xnT_w if (c, kd) == (0, 0) else []))
                        t_tr = em.emit("tensor", lambda e, cc=c, k=kd, ii=i: e.transpose(
                            ps_tr[ii][:, 0:128],
                            xn_o[:, cc * D + k * 128: cc * D + k * 128 + 128], ident[:, :]),
                            waits=tw, inc=sPE)
                        B_ptr[i].set_ready(t_tr)
                        t_ev = em.emit("vector", lambda e, cc=c, k=kd, ii=i: e.tensor_copy(
                            xnT_blk[:, k * W + cc * 128: k * W + cc * 128 + 128], ps_tr[ii][:, 0:128]),
                            waits=[t_tr], inc=sVE, war=(kd != 7))
                        B_ptr[i].add_reader(t_ev)
                    pc = 1
                    gw_w = dedup(B_pch[pc].wr_waits() + [t_ev, t_gw, t_gb, consts_tok])
                    for kd in range(8):
                        em.emit("tensor", lambda e, k=kd, cc=c, p=pc: e.matmul(
                            ps_ch[p][:, 0:NE],
                            xnT_blk[:, k * W + cc * 128: k * W + cc * 128 + 128],
                            gw_sb[:, k * NE:(k + 1) * NE],
                            start=(k == 0), stop=False), waits=gw_w if kd == 0 else ())
                    t_g = em.emit("tensor", lambda e, p=pc: e.matmul(
                        ps_ch[p][:, 0:NE], ones_row[0:1, 0:128], gb_sb[0:1, :],
                        start=False, stop=True), inc=sPE)
                    B_pch[pc].set_ready(t_g)
                    gsw = dedup([t_g, st_prev_tel] + B_gs.wr_waits())
                    em.emit("vector", lambda e, p=pc: e.reduce_max(
                        gs[:, 0:1], ps_ch[p][:, 0:NE], axis=AX.X), waits=gsw)
                    em.emit("vector", lambda e: e.tensor_scalar_mul(gs[:, 1:2], gs[:, 0:1], -1.0), dr=True)
                    t_m1 = em.emit("vector", lambda e, p=pc, cc=c: e.tensor_scalar(
                        m1m2[:, cc * 2 * NE: cc * 2 * NE + NE], ps_ch[p][:, 0:NE], gs[:, 0:1], None, ALU.is_ge),
                        waits=dedup(B_m1m2.wr_waits() if c == 0 else []), inc=sVE)
                    t_el = em.emit("scalar", lambda e, p=pc: e.activation(
                        gt[:, 0:NE], ps_ch[p][:, 0:NE], AF.Exp, bias=gs[:, 1:2]),
                        waits=[t_m1], inc=sSE)
                    st_prev_tel = t_el
                    B_pch[pc].add_reader(t_el)
                    em.emit("vector", lambda e: e.reduce_sum(gs[:, 2:3], gt[:, 0:NE], axis=AX.X),
                            waits=[t_el])
                    em.emit("vector", lambda e, cc=c: e.scalar_tensor_tensor(
                        gt[:, NE:2 * NE], m1m2[:, cc * 2 * NE: cc * 2 * NE + NE], -2.0, gt[:, 0:NE], ALU.mult, ALU.add))
                    em.emit("vector", lambda e: e.reduce_max(gs[:, 4:5], gt[:, NE:2 * NE], axis=AX.X), dr=True)
                    t_m2 = em.emit("vector", lambda e, cc=c: e.tensor_scalar(
                        m1m2[:, cc * 2 * NE + NE: (cc + 1) * 2 * NE], gt[:, NE:2 * NE], gs[:, 4:5], None, ALU.is_ge), inc=sVE, dr=True)
                    em.emit("vector", lambda e: e.reciprocal(gs[:, 3:4], gs[:, 2:3]))
                    t_vs = em.emit("vector", lambda e, col=bi * 4 + c: e.tensor_scalar(
                        vs_all[:, col:col + 1], gs[:, 4:5], 1.0, gs[:, 3:4], ALU.add, ALU.mult), inc=sVE, dr=True)
                    B_gs.set_ready(t_vs)
                    state["m12"][c] = (t_m1, t_m2)
                    B_m1m2.set_ready(t_m1)
                B_xnT.set_ready(t_ev)
                B_xn.add_reader(t_ev)

                t_st = em.emit("scalar", lambda e, ss=s, nn=n: e.dma_start(
                    out=xnT_scr[ss, :, :, nn * W:(nn + 1) * W].rearrange("k p t -> p k t"),
                    in_=xnT_blk[:, :].rearrange("p (k t) -> p k t", t=W)),
                    waits=[B_xnT.ready[0]], inc=sQS)
                B_xnT.add_reader(t_st)

                state["t_xn"] = st_t_xn
                state["tel"] = st_prev_tel


            def emit_prologue_b(s, n):
                bi = s * NBLK + n
                for c in range(4):
                    t_m1, t_m2 = state["m12"][c]
                    for j, (off, mdstT, mbuf) in enumerate(((0, m1T_sb, B_m1T), (NE, m2T_sb, B_m2T))):
                        pt = (c * 2 + j) % 2
                        tw = dedup([t_m1, t_m2, consts_tok] + B_ptr[pt].wr_waits() +
                                   (mbuf.wr_waits() if c == 0 else []))
                        t_tr = em.emit("tensor", lambda e, o=off, p=pt, cc=c: e.transpose(
                            ps_tr[p][0:NE, 0:128], m1m2[:, cc * 2 * NE + o: cc * 2 * NE + o + NE], ident[:, :]),
                            waits=tw, inc=sPE)
                        B_ptr[pt].set_ready(t_tr)
                        t_mev = em.emit("vector", lambda e, dT=mdstT, cc=c, p=pt: e.tensor_copy(
                            dT[:, cc * 128:(cc + 1) * 128], ps_tr[p][0:NE, 0:128]),
                            waits=[t_tr], inc=sVE)
                        B_ptr[pt].add_reader(t_mev)
                        if c == 3:
                            mbuf.set_ready(t_mev)
                        if j == 1:
                            B_m1m2.add_reader(t_tr)
                sw = dedup(B_pch[0].wr_waits() + [B_xnT.ready[0], t_aT])
                t_schain = None
                for kd in range(8):
                    t_schain = em.emit("tensor", lambda e, k=kd: e.matmul(
                        ps_ch[0][:, 0:W], aT_sb[:, k * 128:(k + 1) * 128], xnT_blk[:, k * W:(k + 1) * W],
                        start=(k == 0), stop=(k == 7)),
                        waits=sw if kd == 0 else (), inc=sPE if kd == 7 else None)
                B_pch[0].set_ready(t_schain)
                t_sev = em.emit("vector", lambda e: e.tensor_copy(s_sb[:, :], ps_ch[0][:, 0:W]),
                                waits=dedup([t_schain] + B_s.wr_waits()), inc=sVE)
                B_s.set_ready(t_sev)
                B_pch[0].add_reader(t_sev)
                for j, (mT, mbuf, scr) in enumerate(
                        ((m1T_sb, B_m1T, s1_scr), (m2T_sb, B_m2T, s2_scr))):
                    dst = s12_sb if j == 0 else s_sb
                    dbuf2 = B_s12 if j == 0 else B_s
                    rw = dedup(B_pch[1].wr_waits() + mbuf.ready + [consts_tok])
                    t_rep = em.emit("tensor", lambda e, mm=mT: e.matmul(
                        ps_ch[1][:, 0:W], rep_sb[:, :], mm[:, :], start=True, stop=True),
                        waits=rw, inc=sPE)
                    B_pch[1].set_ready(t_rep)
                    mbuf.add_reader(t_rep)
                    t_s12 = em.emit("vector", lambda e, d=dst: e.tensor_tensor(
                        d[:, 0:W], s_sb[:, :], ps_ch[1][:, 0:W], ALU.mult),
                        waits=dedup([t_rep, t_sev] + (B_s12.wr_waits() if j == 0 else [])), inc=sVE)
                    B_pch[1].add_reader(t_s12)
                    B_s.add_reader(t_s12)
                    dbuf2.set_ready(t_s12)
                    t_st = em.emit("scalar", lambda e, d=dst, sc=scr, ss=s, nn=n: e.dma_start(
                        out=sc[:, ss * T + nn * W: ss * T + (nn + 1) * W],
                        in_=d[:, 0:W]), waits=[t_s12], inc=sQS)
                    dbuf2.add_reader(t_st)




            kT_toks = [[None] * 8, [None] * 8]

            emit_prologue_a1(0, 0)
            emit_prologue_a2(0, 0)
            emit_prologue_b(0, 0)
            for s in range(SEQ):
                for n in range(NBLK):
                    bi = s * NBLK + n
                    par = bi % 2
                    bpar = n % 2
                    qoT_blk = qoT_blk2[par]
                    B_qoT = B_qoT2[par]

                    qT_toks = [None] * 8
                    kT_toks_cur = [None] * 8
                    for m in range(8):
                        for which, (wsb, wtk, scale) in enumerate(
                                ((wq_sb, wq_tok, 0.125), (wk_sb, wk_tok, None))):
                            dstT = qoT_blk if which == 0 else kT_blk[bpar]
                            dbuf = B_qoT if which == 0 else B_kT[bpar]
                            if m == 0:
                                if which == 0:
                                    qoT_w0 = dedup(dbuf.wr_waits())
                                else:
                                    kT_w0 = dedup(dbuf.wr_waits())
                            dw = qoT_w0 if which == 0 else kT_w0
                            pc = (m * 2 + which) % 2
                            mw = dedup(B_pch[pc].wr_waits() + [B_xnT.ready[0], wtk])
                            t_mm = None
                            for kd in range(8):
                                t_mm = em.emit("tensor", lambda e, k=kd, mm=m, p=pc, ws=wsb: e.matmul(
                                    ps_ch[p][:, 0:W], ws[:, k * D + mm * 128: k * D + mm * 128 + 128],
                                    xnT_blk[:, k * W:(k + 1) * W],
                                    start=(k == 0), stop=(k == 7)),
                                    waits=mw if kd == 0 else (), inc=sPE if kd == 7 else None)
                            B_pch[pc].set_ready(t_mm)
                            if scale is not None:
                                t_ev = em.emit("vector", lambda e, mm=m, p=pc, dT=dstT: e.tensor_scalar_mul(
                                    dT[:, mm * W:(mm + 1) * W], ps_ch[p][:, 0:W], 0.125),
                                    waits=dedup([t_mm] + (dw if m == 0 else [])), inc=sVE)
                                qT_toks[m] = t_ev
                            else:
                                t_ev = em.emit("vector", lambda e, mm=m, p=pc, dT=dstT: e.tensor_copy(
                                    dT[:, mm * W:(mm + 1) * W], ps_ch[p][:, 0:W]),
                                    waits=dedup([t_mm] + (dw if m == 0 else [])), inc=sVE)
                                kT_toks_cur[m] = t_ev
                            B_pch[pc].add_reader(t_ev)
                    B_qoT.set_ready(qT_toks[7])
                    B_kT[bpar].set_ready(kT_toks_cur[7])
                    kT_toks[bpar] = kT_toks_cur
                    vg_w = dedup(B_vaug[bpar].wr_waits())
                    t_vev = None
                    for c in range(4):
                        for hf in range(2):
                            pc = (c * 2 + hf) % 2
                            mw = dedup(B_pch[pc].wr_waits() + [B_xnT.ready[0], wv_tok])
                            t_mm = None
                            for kd in range(8):
                                t_mm = em.emit("tensor", lambda e, k=kd, cc=c, h=hf, p=pc: e.matmul(
                                    ps_ch[p][:, 0:W],
                                    xnT_blk[:, k * W + cc * 128: k * W + cc * 128 + 128],
                                    wv_sb[:, k * D + h * W: k * D + (h + 1) * W],
                                    start=(k == 0), stop=(k == 7)),
                                    waits=mw if kd == 0 else (), inc=sPE if kd == 7 else None)
                            B_pch[pc].set_ready(t_mm)

                            def vev(e, cc=c, h=hf, p=pc, bp=bpar):
                                base = cc * H * 65 + h * 8 * 65
                                dst = vaug[bp][:, base: base + 8 * 65]
                                dstv = dst.rearrange("p (hh x) -> p hh x", x=65)[:, :, 0:64]
                                srcv = ps_ch[p][:, 0:W].rearrange("p (hh x) -> p hh x", x=64)
                                return e.tensor_copy(dstv, srcv)
                            t_vev = em.emit("vector", vev,
                                            waits=dedup([t_mm] + (vg_w if (c, hf) == (0, 0) else [])),
                                            inc=sVE, war=((c, hf) != (3, 1)))
                            B_pch[pc].add_reader(t_vev)
                    B_vaug[bpar].set_ready(t_vev)
                    B_xnT.add_reader(t_vev)
                    if bi > 0:
                        emit_prologue_b(s, n)

                    sc_pool = [ps_sc[0], ps_sc[1], ps_ch[0]]
                    sc_bufs = [B_psc[0], B_psc[1], B_pch[0]]
                    o_w = dedup(B_o.wr_waits())
                    rcp_w = [[] for _ in range(8)]
                    msk_tok = [None, None, None]
                    last_o = [None]

                    def emit_scores(h, pv_h=None, n=n, bpar=bpar):
                        hp = h % 3
                        rr = sc_ranges(n)
                        pace = max(1, len(rr) // 4)
                        hb = (h % 2) * 64
                        ck = h // 2
                        ew = dedup(B_expT[hp].wr_waits())
                        t_exp = None
                        t_msk = None
                        for ri, (kind, jc, i0, wdt) in enumerate(rr):
                            slot = jc if kind == "prev" else 4 + jc
                            kbuf = kT_blk[(n - 1) % 2] if kind == "prev" else kT_blk[bpar]
                            ck2 = h // 2
                            if kind == "prev":
                                ktoks = [kT_toks[(n - 1) % 2][ck2]]
                            else:
                                ktoks = [kT_toks[bpar][ck2]]
                            psc = ri % 3
                            pbank = sc_pool[psc]
                            pbuf = sc_bufs[psc]
                            mw = dedup(pbuf.wr_waits() + ktoks + [qT_toks[ck2]])
                            t_mm = em.emit("tensor", lambda e, kb=kbuf, b=hb, c=ck, j=jc, ii=i0, wd=wdt, pb=pbank, qb=qoT_blk: e.matmul(
                                pb[:, ii:ii + wd],
                                kb[b:b + 64, c * W + j * 128: c * W + j * 128 + 128],
                                qb[b:b + 64, c * W + ii: c * W + ii + wd],
                                start=True, stop=True),
                                waits=mw, inc=sPE)
                            pbuf.set_ready(t_mm)
                            first = (kind, jc) == (("own", 0) if n == 0 else ("prev", 0))
                            t_exp = em.emit("scalar", lambda e, sl=slot, ii=i0, wd=wdt, pb=pbank, hh=hp: e.activation(
                                expT[hh][:, sl * W + ii: sl * W + ii + wd],
                                pb[:, ii:ii + wd], AF.Exp, bias=fc_sb[:, 0:1]),
                                waits=dedup([t_mm, t_fc] + (ew if first else [])),
                                inc=sSE, war=(ri != len(rr) - 1))
                            pbuf.add_reader(t_exp)
                            if pv_h is not None and ri % pace == pace - 1 and ri // pace < 4:
                                emit_pv_ic(pv_h, ri // pace)
                        for mi, (kind, jc, i0, wdt) in enumerate(sc_ranges(n)):
                            slot = jc if kind == "prev" else 4 + jc
                            mk = maskA if kind == "prev" else maskB
                            dcol = slot * W + jc * 128
                            t_msk = em.emit("gpsimd", lambda e, dc=dcol, m=mk, hh=hp: e.tensor_mul(
                                expT[hh][:, dc:dc + 128], expT[hh][:, dc:dc + 128], m[:, :]),
                                waits=dedup([t_exp, consts_tok]),
                                inc=sGP if mi == len(sc_ranges(n)) - 1 else None)
                        B_expT[hp].set_ready(t_msk)
                        return t_msk

                    def emit_pv_ic(h, ic, n=n, bpar=bpar, last=False):
                        hp = h % 3
                        t_msk = msk_tok[hp]
                        if True:
                            tiles = ([("prev", jc) for jc in range(ic, 4)] if n > 0 else []) + \
                                    [("own", jc) for jc in range(0, ic + 1)]
                            pp = ic % 2
                            mw = dedup(B_ppv[pp].wr_waits() + [t_msk] +
                                       B_vaug[bpar].ready + (B_vaug[(n - 1) % 2].ready if n > 0 else []))
                            t_pv = None
                            for ti, (kind, jc) in enumerate(tiles):
                                slot = jc if kind == "prev" else 4 + jc
                                vb = vaug[(n - 1) % 2] if kind == "prev" else vaug[bpar]
                                t_pv = em.emit("tensor", lambda e, sl=slot, j=jc, icc=ic, v=vb, hh=hp, p=pp, hd=h,
                                               st=(ti == 0), sp=(ti == len(tiles) - 1): e.matmul(
                                    ps_pv[p][:, 0:65],
                                    expT[hh][:, sl * W + icc * 128: sl * W + icc * 128 + 128],
                                    v[:, j * H * 65 + hd * 65: j * H * 65 + hd * 65 + 65],
                                    start=st, stop=sp),
                                    waits=mw if ti == 0 else (),
                                    inc=sPE if ti == len(tiles) - 1 else None)
                            B_ppv[pp].set_ready(t_pv)
                            B_vaug[bpar].add_reader(t_pv)
                            if n > 0:
                                B_vaug[(n - 1) % 2].add_reader(t_pv)
                            rc = (h * 4 + ic) % 8
                            em.emit("vector", lambda e, p=pp, r=rc: e.reciprocal(
                                rcp[:, r:r + 1], ps_pv[p][:, 64:65]),
                                waits=dedup([t_pv] + (o_w if (h, ic) == (0, 0) else [])))
                            t_o = em.emit("vector", lambda e, icc=ic, hd=h, p=pp, r=rc: e.tensor_scalar(
                                o_sb[:, icc * D + hd * 64: icc * D + hd * 64 + 64],
                                ps_pv[p][:, 0:64], rcp[:, r:r + 1], None, ALU.mult),
                                inc=sVE, dr=True, war=(not (last and ic == 3)))
                            B_ppv[pp].add_reader(t_o)
                            B_expT[hp].add_reader(t_o)
                            last_o[0] = t_o

                    def emit_pv(h, last=False):
                        for ic in range(4):
                            emit_pv_ic(h, ic, last=last)

                    for h in range(H):
                        msk_tok[h % 3] = emit_scores(h, pv_h=(h - 2 if h >= 2 else None))
                        if h % 2 == 1 and h < 8 and state.get("proj") is not None:
                            state["proj"](h // 2)
                            if h == 7:
                                state["proj"] = None
                        if h == 2 and bi + 1 < SEQ * NBLK:
                            emit_prologue_a1((bi + 1) // NBLK, (bi + 1) % NBLK)
                        if h == 12 and bi + 1 < SEQ * NBLK:
                            emit_prologue_a2((bi + 1) // NBLK, (bi + 1) % NBLK)
                    emit_pv(H - 2)
                    emit_pv(H - 1, last=True)
                    t_o = last_o[0]
                    B_o.set_ready(t_o)
                    t_attn = (sPE, sPE.n)
                    B_qoT.add_reader(t_attn)
                    B_kT[0].add_reader(t_attn)
                    B_kT[1].add_reader(t_attn)

                    qw = dedup(B_qoT.wr_waits())
                    t_ev = None
                    for c in range(4):
                        for kd in range(8):
                            i = (c * 8 + kd) % 2
                            tw = dedup([B_o.ready[0], consts_tok] + B_ptr[i].wr_waits() +
                                       (qw if (c, kd) == (0, 0) else []))
                            t_tr = em.emit("tensor", lambda e, cc=c, k=kd, ii=i: e.transpose(
                                ps_tr[ii][:, 0:128],
                                o_sb[:, cc * D + k * 128: cc * D + k * 128 + 128], ident[:, :]),
                                waits=tw, inc=sPE)
                            B_ptr[i].set_ready(t_tr)
                            t_ev = em.emit("vector", lambda e, cc=c, k=kd, ii=i, qb=qoT_blk: e.tensor_copy(
                                qb[:, k * W + cc * 128: k * W + cc * 128 + 128], ps_tr[ii][:, 0:128]),
                                waits=[t_tr], inc=sVE, war=((c, kd) != (3, 7)))
                            B_ptr[i].add_reader(t_ev)
                    B_qoT.set_ready(t_ev)
                    B_o.add_reader(t_ev)

                    def emit_proj_c(c, ss=s, nn=n, qb=qoT_blk, Bq=B_qoT):
                        rp = c % 2
                        t_xr = em.emit("gpsimd", lambda e, cc=c, ss=ss, nn=nn, p=rp: e.dma_start(
                            out=x_res[p][:, :],
                            in_=x_in[ss * T + nn * W + cc * 128: ss * T + nn * W + (cc + 1) * 128, :]),
                            waits=dedup(B_xres[rp].wr_waits()), inc=sQG)
                        B_xres[rp].set_ready(t_xr)
                        ap_par = c % 2
                        apw = dedup(B_attnp[ap_par].wr_waits())
                        t_ap = None
                        for hf in range(2):
                            pc = 1
                            mw = dedup(B_pch[pc].wr_waits() + Bq.ready + [wo_tok])
                            t_mm = None
                            for kd in range(8):
                                t_mm = em.emit("tensor", lambda e, k=kd, cc=c, h=hf, p=pc, qb=qb: e.matmul(
                                    ps_ch[p][:, 0:W],
                                    qb[:, k * W + cc * 128: k * W + cc * 128 + 128],
                                    wo_sb[:, k * D + h * W: k * D + (h + 1) * W],
                                    start=(k == 0), stop=(k == 7)),
                                    waits=mw if kd == 0 else (), inc=sPE if kd == 7 else None)
                            B_pch[pc].set_ready(t_mm)
                            Bq.add_reader(t_mm)
                            t_ap = em.emit("vector", lambda e, h=hf, p=pc, ap=ap_par, rp=rp: e.tensor_tensor(
                                attnp_ev[ap][:, h * W:(h + 1) * W], ps_ch[p][:, 0:W],
                                x_res[rp][:, h * W:(h + 1) * W], ALU.add),
                                waits=dedup([t_mm, t_xr] + (apw if hf == 0 else [])), inc=sVE)
                            B_pch[pc].add_reader(t_ap)
                            B_xres[rp].add_reader(t_ap)
                        B_attnp[ap_par].set_ready(t_ap)
                        t_st = em.emit("scalar", lambda e, cc=c, ss=ss, nn=nn, ap=ap_par: e.dma_start(
                            out=attnp_scr[ss * T + nn * W + cc * 128: ss * T + nn * W + (cc + 1) * 128, :],
                            in_=attnp_ev[ap][:, :]), waits=[t_ap], inc=sQS)
                        B_attnp[ap_par].add_reader(t_st)

                    if bi == SEQ * NBLK - 1:
                        for c in range(4):
                            emit_proj_c(c)
                    else:
                        state["proj"] = emit_proj_c

            feA = [(sPE, sPE.n), (sVE, sVE.n), (sSE, sSE.n), (sQS, sQS.n), (sQL, sQL.n)]

        pb = contextlib.ExitStack()
        with pb:
            wi_sb = pb.enter_context(nc.sbuf_tensor("wi_sb", [128, 8 * DFF], BF16))
            wio_sb = pb.enter_context(nc.sbuf_tensor("wio_sb", [128, NFC * D], BF16))
            bst_sb = pb.enter_context(nc.sbuf_tensor("bst_sb", [128, DFF], BF16))
            xnT_f = [pb.enter_context(nc.sbuf_tensor(f"xnT_f{i}", [128, 8 * TT], BF16)) for i in range(2)]
            s1f = [pb.enter_context(nc.sbuf_tensor(f"s1f{i}", [128, TT], BF16)) for i in range(2)]
            s2f = [pb.enter_context(nc.sbuf_tensor(f"s2f{i}", [128, TT], BF16)) for i in range(2)]
            hsum = pb.enter_context(nc.sbuf_tensor("hsum", [128, NFC * TT], BF16))
            attnp_f = [pb.enter_context(nc.sbuf_tensor(f"attnp_f{i}", [128, 2 * D], F32)) for i in range(2)]

            ps_z = [pb.enter_context(nc.psum_tensor(f"ps_z{i}", [128, 512], F32)) for i in range(4)]
            ps_wo = [pb.enter_context(nc.psum_tensor(f"ps_wo{i}", [128, 512], F32)) for i in range(4)]

            t_wo2 = None
            wi_toks = []
            for k4 in range(2):
                wi_toks.append(em.emit("gpsimd", lambda e, k=k4: e.dma_start(
                    out=wi_sb[:, k * 4 * DFF:(k + 1) * 4 * DFF].rearrange("p (c d) -> p c d", d=DFF),
                    in_=wiT_in[k * 512:(k + 1) * 512, :].rearrange("(c p) d -> p c d", p=128)),
                    waits=feA if k4 == 0 else (), inc=sQG))
            t_wi = wi_toks[1]
            for fk in range(8):
                t_wo2 = em.emit("gpsimd", lambda e, k=fk: e.dma_start(
                    out=wio_sb[:, k * 4 * D:(k + 1) * 4 * D].rearrange("p (c d) -> p c d", d=D),
                    in_=wioT_in[k * 512:(k + 1) * 512, :].rearrange("(c p) d -> p c d", p=128)), inc=sQG)
            t_bst = em.emit("gpsimd", lambda e: e.dma_start(out=bst_sb[:, :], in_=bst_in[:, :]), inc=sQG)
            t_bi = consts_tok
            h2_sb = [pb.enter_context(nc.sbuf_tensor(f"h2_sb{i}", [128, TT], BF16)) for i in range(2)]
            base_bf = [pb.enter_context(nc.sbuf_tensor(f"base_bf{i}", [128, TT], BF16)) for i in range(2)]
            B_h2 = [Buf(), Buf()]
            B_base = [Buf(), Buf()]

            B_xf = [Buf(), Buf()]
            B_s1 = [Buf(), Buf()]
            B_s2 = [Buf(), Buf()]
            B_hs = Buf()
            B_af = [Buf(), Buf()]
            B_z = [Buf() for _ in range(4)]
            B_w = [Buf() for _ in range(4)]

            for tl in range(NTILE):
                s = tl // (T // TT)
                t0 = (tl % (T // TT)) * TT
                par = tl % 2

                t_lx = em.emit("sync", lambda e, ss=s, tt=t0, p=par: e.dma_start(
                    out=xnT_f[p][:, :].rearrange("p (k t) -> p k t", t=TT),
                    in_=xnT_scr[ss, :, :, tt:tt + TT].rearrange("k p t -> p k t")),
                    waits=dedup(B_xf[par].wr_waits() + (feA if tl == 0 else [])), inc=sQL)
                B_xf[par].set_ready(t_lx)
                t_l1 = em.emit("sync", lambda e, ss=s, tt=t0, p=par: e.dma_start(
                    out=s1f[p][:, :], in_=s1_scr[:, ss * T + tt: ss * T + tt + TT]),
                    waits=dedup(B_s1[par].wr_waits() + (feA if tl == 0 else [])), inc=sQL)
                B_s1[par].set_ready(t_l1)
                t_l2 = em.emit("sync", lambda e, ss=s, tt=t0, p=par: e.dma_start(
                    out=s2f[p][:, :], in_=s2_scr[:, ss * T + tt: ss * T + tt + TT]),
                    waits=dedup(B_s2[par].wr_waits() + (feA if tl == 0 else [])), inc=sQL)
                B_s2[par].set_ready(t_l2)
                t_la = em.emit("sync", lambda e, ss=s, tt=t0, p=par: e.dma_start(
                    out=attnp_f[p][:, :].rearrange("p (c d) -> p c d", d=D),
                    in_=attnp_scr[ss * T + tt: ss * T + tt + TT, :].rearrange("(c p) d -> p c d", p=128)),
                    waits=dedup(B_af[par].wr_waits() + (feA if tl == 0 else [])), inc=sQL)
                B_af[par].set_ready(t_la)

                hs_w = dedup(B_hs.wr_waits())
                t_h2 = None

                def emit_base(fc):
                    fp = fc % 2
                    bb = fp * 3
                    mw = dedup(B_z[bb].wr_waits() + [t_lx, wi_toks[0]])
                    for kd in range(8):
                        em.emit("tensor", lambda e, k=kd, f=fc, z=bb, p=par: e.matmul(
                            ps_z[z][:, 0:TT],
                            wi_sb[:, k * DFF + f * 128: k * DFF + f * 128 + 128],
                            xnT_f[p][:, k * TT:(k + 1) * TT],
                            start=(k == 0), stop=(k == 7)),
                            waits=mw if kd == 0 else ([wi_toks[1]] if kd == 4 else ()),
                            inc=sPE if kd == 7 else None)
                    t_bmm = (sPE, sPE.n)
                    B_z[bb].set_ready(t_bmm)
                    t_bev = em.emit("vector", lambda e, z=bb, p=fp: e.tensor_copy(
                        base_bf[p][:, :], ps_z[z][:, 0:TT]),
                        waits=dedup([t_bmm] + B_base[fp].wr_waits()), inc=sVE)
                    B_z[bb].add_reader(t_bev)
                    B_base[fp].set_ready(t_bev)

                def emit_z(fc):
                    fp = fc % 2
                    for which, (sfb, stok, lb) in enumerate(((s1f[par], t_l1, 1), (s2f[par], t_l2, 2))):
                        em.emit("tensor", lambda e, p=fp, z=lb: e.matmul(
                            ps_z[z][:, 0:TT], ident[:, :], base_bf[p][:, :], start=True, stop=False),
                            waits=dedup(B_z[lb].wr_waits() + B_base[fp].ready + [consts_tok]))
                        t_z = em.emit("tensor", lambda e, f=fc, z=lb, sf=sfb: e.matmul(
                            ps_z[z][:, 0:TT], bst_sb[:, f * 128:(f + 1) * 128], sf[:, :],
                            start=False, stop=True),
                            waits=dedup([stok, t_bst]), inc=sPE)
                        B_z[lb].set_ready(t_z)
                        if which == 1:
                            B_base[fp].add_reader(t_z)
                    t_h1 = em.emit("scalar", lambda e, f=fc: e.activation(
                        hsum[:, f * TT:(f + 1) * TT], ps_z[1][:, 0:TT], AF.Relu,
                        bias=bi2_sb[:, f:f + 1]),
                        waits=dedup([B_z[1].ready[0], t_bi] + (hs_w if fc == 0 else [])))
                    t_hb = em.emit("scalar", lambda e, f=fc, p=fp: e.activation(
                        h2_sb[p][:, :], ps_z[2][:, 0:TT], AF.Relu, bias=bi2_sb[:, f:f + 1]),
                        waits=dedup([B_z[2].ready[0]] + B_h2[fp].wr_waits()), inc=sSE)
                    B_z[1].add_reader(t_hb)
                    B_z[2].add_reader(t_hb)
                    B_h2[fp].set_ready(t_hb)
                    t = em.emit("vector", lambda e, f=fc, p=fp: e.tensor_tensor(
                        hsum[:, f * TT:(f + 1) * TT], hsum[:, f * TT:(f + 1) * TT],
                        h2_sb[p][:, :], ALU.add),
                        waits=dedup([t_hb]), inc=sVE, war=(fc != NFC - 1))
                    B_h2[fp].add_reader(t)
                    return t

                h2_toks = [None] * NFC
                for fc in range(NFC + 1):
                    if fc < NFC:
                        emit_base(fc)
                    if fc >= 1:
                        t_h2 = emit_z(fc - 1)
                        h2_toks[fc - 1] = t_h2
                B_hs.set_ready(t_h2)
                B_xf[par].add_reader((sPE, sPE.n))
                B_s1[par].add_reader((sPE, sPE.n))
                B_s2[par].add_reader((sPE, sPE.n))

                t_wo_done = None
                for tsub in range(2):
                    for dh in range(2):
                        wb = tsub * 2 + dh
                        mw = dedup([t_wo2, t_bo2] + B_w[wb].wr_waits())
                        for fc in range(NFC):
                            em.emit("tensor", lambda e, f=fc, ts=tsub, d=dh, b=wb: e.matmul(
                                ps_wo[b][:, 0:512],
                                hsum[:, f * TT + ts * 128: f * TT + ts * 128 + 128],
                                wio_sb[:, f * D + d * 512: f * D + (d + 1) * 512],
                                start=(f == 0), stop=False),
                                waits=dedup((mw if fc == 0 else []) + [h2_toks[min(fc | 3, NFC - 1)]]))
                        t_wo_done = em.emit("tensor", lambda e, d=dh, b=wb: e.matmul(
                            ps_wo[b][:, 0:512], ones_row[0:1, 0:128], bo2_sb[0:1, d * 512:(d + 1) * 512],
                            start=False, stop=True), inc=sPE)
                        B_w[wb].set_ready(t_wo_done)
                B_hs.add_reader(t_wo_done)

                t_ev = None
                for tsub in range(2):
                    col = s * 16 + (t0 // 128) + tsub
                    for dh in range(2):
                        wb = tsub * 2 + dh
                        t_ev = em.emit("vector", lambda e, ts=tsub, d=dh, b=wb, p=par, cc=col: e.scalar_tensor_tensor(
                            attnp_f[p][:, ts * D + d * 512: ts * D + (d + 1) * 512],
                            ps_wo[b][:, 0:512], vs_all[:, cc:cc + 1],
                            attnp_f[p][:, ts * D + d * 512: ts * D + (d + 1) * 512],
                            ALU.mult, ALU.add),
                            waits=dedup(B_w[wb].ready + [t_la]), inc=sVE)
                        B_w[wb].add_reader(t_ev)
                B_af[par].add_reader(t_ev)
                t_st = em.emit("scalar", lambda e, ss=s, tt=t0, p=par: e.dma_start(
                    out=out_hbm[ss * T + tt: ss * T + tt + TT, :].rearrange("(c p) d -> p c d", p=128),
                    in_=attnp_f[p][:, :].rearrange("p (c d) -> p c d", d=D)),
                    waits=[t_ev], inc=sQS)
                B_af[par].add_reader(t_st)

            final_store = (sQS, sQS.n)

        em.emit("sync", lambda e: e.nop(), waits=[final_store])

        with nc.Block() as block:
            @block.sync
            def _(sync):
                for op in em.ops["sync"]:
                    op(sync)

            @block.gpsimd
            def _(gpsimd):
                for op in em.ops["gpsimd"]:
                    op(gpsimd)

            @block.tensor
            def _(tensor):
                for op in em.ops["tensor"]:
                    op(tensor)

            @block.vector
            def _(vector):
                for op in em.ops["vector"]:
                    op(vector)

            @block.scalar
            def _(scalar):
                for op in em.ops["scalar"]:
                    op(scalar)

    return nc


_BF = ml_dtypes.bfloat16


def _consts():
    ident = np.eye(128, dtype=_BF)
    jl = np.arange(128)[:, None]
    il = np.arange(128)[None, :]
    maskA = (jl > il).astype(_BF)
    maskB = (jl <= il).astype(_BF)
    ones_row = np.ones((1, 512), dtype=_BF)
    rep = np.kron(np.eye(NE, dtype=np.float32), np.ones((1, RK), dtype=np.float32)).astype(_BF)
    return ident, maskA, maskB, ones_row, rep


def kernel(L0_attn_x, L0_attn_Wq, L0_attn_Wk, L0_attn_Wv, L0_attn_Wo,
           L0_ffn_wi, L0_ffn_bi, L0_ffn_wo, L0_ffn_bo,
           L0_ffn_lora_A, L0_ffn_lora_B, L0_ffn_gate_w, L0_ffn_gate_b):
    x = np.asarray(L0_attn_x, dtype=np.float32)
    shared = dict(
        wqT=np.ascontiguousarray(np.asarray(L0_attn_Wq, np.float32).T),
        wkT=np.ascontiguousarray(np.asarray(L0_attn_Wk, np.float32).T),
        wvT=np.ascontiguousarray(np.asarray(L0_attn_Wv, np.float32).T),
        woT=np.ascontiguousarray(np.asarray(L0_attn_Wo, np.float32).T),
        wiT=np.ascontiguousarray(np.asarray(L0_ffn_wi, np.float32).T),
        wioT=np.ascontiguousarray(np.asarray(L0_ffn_wo, np.float32).T),
        bi2=np.ascontiguousarray(np.asarray(L0_ffn_bi, np.float32).reshape(NFC, 128).T),
        bo=np.asarray(L0_ffn_bo, np.float32).reshape(1, D),
        aT=np.ascontiguousarray(np.asarray(L0_ffn_lora_A, np.float32).transpose(2, 0, 1).reshape(D, NE * RK)),
        bst=np.ascontiguousarray(np.asarray(L0_ffn_lora_B, np.float32).transpose(0, 2, 1).reshape(NE * RK, DFF)),
        gwT=np.ascontiguousarray(np.asarray(L0_ffn_gate_w, np.float32).T),
        gb=np.asarray(L0_ffn_gate_b, np.float32).reshape(1, NE),
    )
    ident, maskA, maskB, ones_row, rep = _consts()
    shared.update(ident=ident, maskA=maskA, maskB=maskB, ones_row=ones_row, rep=rep)

    nc = build_program()
    in_maps = []
    for c in range(NCORES):
        m = dict(shared)
        m["x"] = np.ascontiguousarray(x[c * SEQ:(c + 1) * SEQ].reshape(SEQ * T, D))
        in_maps.append(m)
    res = run_bass_kernel_spmd(nc, in_maps, core_ids=list(range(NCORES)))
    out = np.empty((NCORES * SEQ, T, D), dtype=np.float32)
    for c in range(NCORES):
        out[c * SEQ:(c + 1) * SEQ] = np.asarray(res.results[c]["out"]).reshape(SEQ, T, D)
    return out




# version 22
# speedup vs baseline: 1.0633x; 1.0633x over previous
import contextlib
import numpy as np
import ml_dtypes
import concourse.bass as bass
import concourse.mybir as mybir
from concourse.bass_utils import run_bass_kernel_spmd

F32 = mybir.dt.float32
BF16 = mybir.dt.bfloat16
AF = mybir.ActivationFunctionType
ALU = mybir.AluOpType
AX = mybir.AxisListType

NCORES = 8
SEQ = 4
T = 2048
D = 1024
H = 16
DH = 64
W = 512
NBLK = 4
DFF = 4096
NE = 8
RK = 16
TT = 256
NFC = DFF // 128
NTILE = SEQ * T // TT
EXP_BIAS = -12.0

ENGS = ("sync", "gpsimd", "tensor", "vector", "scalar")


class Sem:
    def __init__(self, handle, step, eng):
        self.h = handle
        self.step = step
        self.eng = eng
        self.n = 0

    def tok(self):
        self.n += self.step
        return (self, self.n)


class Emitter:
    def __init__(self):
        self.ops = {e: [] for e in ENGS}
        self.satisfied = {e: {} for e in ENGS}

    def emit(self, eng, fn, waits=(), inc=None, dr=False, war=False):
        need = []
        sat = self.satisfied[eng]
        if inc is not None and inc.step == 16:
            waits = list(waits) + [(inc, inc.n)]
        for wt in waits:
            if wt is None:
                continue
            sem, val = wt
            if sem.eng == eng and sem.step == 1:
                continue
            if sat.get(id(sem), 0) >= val:
                continue
            need.append((sem, val))
            sat[id(sem)] = val
        tok = inc.tok() if inc is not None else None
        step = inc.step if inc is not None else 0
        drain = (inc is not None and inc.step == 1 and not war
                 and eng in ("vector", "scalar"))

        def run(e, _need=tuple(need), _tok=tok, _fn=fn, _step=step, _dr=drain, _pre=dr):
            for (sem, val) in _need:
                e.wait_ge(sem.h, val)
            if _pre:
                e.drain()
            ins = _fn(e)
            if _tok is not None:
                if _dr:
                    e.drain().then_inc(_tok[0].h, _step)
                else:
                    ins.then_inc(_tok[0].h, _step)

        self.ops[eng].append(run)
        return tok


class Buf:
    def __init__(self):
        self.ready = []
        self.readers = []

    def wr_waits(self):
        return list(self.ready) + list(self.readers)

    def set_ready(self, tok):
        self.ready = [tok] if tok is not None else []
        self.readers = []

    def add_reader(self, tok):
        if tok is not None:
            self.readers.append(tok)


def dedup(waits):
    best = {}
    for w in waits:
        if w is None:
            continue
        sem, val = w
        k = id(sem)
        if k not in best or best[k][1] < val:
            best[k] = (sem, val)
    return list(best.values())


def sc_ranges(n):
    rr = []
    if n > 0:
        for jc in range(4):
            rr.append(("prev", jc, 0, min(W, 128 * (jc + 1))))
    for jc in range(4):
        rr.append(("own", jc, 128 * jc, W - 128 * jc))
    return rr


def build_program(dbg=False):
    nc = bass.Bass("TRN2", target_bir_lowering=False)
    scr_kind = dict(kind="ExternalOutput") if dbg else {}

    x_in = nc.dram_tensor("x", [SEQ * T, D], F32, kind="ExternalInput").ap()
    wqT = nc.dram_tensor("wqT", [D, D], F32, kind="ExternalInput").ap()
    wkT = nc.dram_tensor("wkT", [D, D], F32, kind="ExternalInput").ap()
    wvT = nc.dram_tensor("wvT", [D, D], F32, kind="ExternalInput").ap()
    woT = nc.dram_tensor("woT", [D, D], F32, kind="ExternalInput").ap()
    wiT_in = nc.dram_tensor("wiT", [D, DFF], F32, kind="ExternalInput").ap()
    wioT_in = nc.dram_tensor("wioT", [DFF, D], F32, kind="ExternalInput").ap()
    bi2_in = nc.dram_tensor("bi2", [128, NFC], F32, kind="ExternalInput").ap()
    bo_in = nc.dram_tensor("bo", [1, D], F32, kind="ExternalInput").ap()
    aT_in = nc.dram_tensor("aT", [D, NE * RK], F32, kind="ExternalInput").ap()
    bst_in = nc.dram_tensor("bst", [NE * RK, DFF], F32, kind="ExternalInput").ap()
    gwT_in = nc.dram_tensor("gwT", [D, NE], F32, kind="ExternalInput").ap()
    gb_in = nc.dram_tensor("gb", [1, NE], F32, kind="ExternalInput").ap()
    ident_in = nc.dram_tensor("ident", [128, 128], BF16, kind="ExternalInput").ap()
    maskA_in = nc.dram_tensor("maskA", [128, 128], BF16, kind="ExternalInput").ap()
    maskB_in = nc.dram_tensor("maskB", [128, 128], BF16, kind="ExternalInput").ap()
    ones_in = nc.dram_tensor("ones_row", [1, 512], BF16, kind="ExternalInput").ap()
    rep_in = nc.dram_tensor("rep", [NE, 128], BF16, kind="ExternalInput").ap()

    out_hbm = nc.dram_tensor("out", [SEQ * T, D], F32, kind="ExternalOutput").ap()

    if dbg:
        dbgx = nc.dram_tensor("dbgx", [128, 4 * D], F32, kind="ExternalOutput").ap()
        dbgxn = nc.dram_tensor("dbgxn", [128, 4 * D], BF16, kind="ExternalOutput").ap()
    xnT_scr = nc.dram_tensor("xnT_scr", [SEQ, 8, 128, T], BF16, **scr_kind).ap()
    s1_scr = nc.dram_tensor("s1_scr", [128, SEQ * T], BF16, **scr_kind).ap()
    s2_scr = nc.dram_tensor("s2_scr", [128, SEQ * T], BF16, **scr_kind).ap()
    attnp_scr = nc.dram_tensor("attnp_scr", [SEQ * T, D], F32, **scr_kind).ap()

    em = Emitter()
    stack = contextlib.ExitStack()
    with stack:
        sQL = Sem(stack.enter_context(nc.semaphore("sQL")), 16, "sync")
        sQC = Sem(stack.enter_context(nc.semaphore("sQC")), 16, "sync")
        sQC2 = Sem(stack.enter_context(nc.semaphore("sQC2")), 16, "gpsimd")
        sQG = Sem(stack.enter_context(nc.semaphore("sQG")), 16, "gpsimd")
        sQW = Sem(stack.enter_context(nc.semaphore("sQW")), 16, "gpsimd")
        sQS = Sem(stack.enter_context(nc.semaphore("sQS")), 16, "scalar")
        sPE = Sem(stack.enter_context(nc.semaphore("sPE")), 1, "tensor")
        sGP = Sem(stack.enter_context(nc.semaphore("sGP")), 1, "gpsimd")
        sVE = Sem(stack.enter_context(nc.semaphore("sVE")), 1, "vector")
        sSE = Sem(stack.enter_context(nc.semaphore("sSE")), 1, "scalar")

        ident = stack.enter_context(nc.sbuf_tensor("ident_sb", [128, 128], BF16))
        maskA = stack.enter_context(nc.sbuf_tensor("maskA_sb", [128, 128], BF16))
        maskB = stack.enter_context(nc.sbuf_tensor("maskB_sb", [128, 128], BF16))
        ones_row = stack.enter_context(nc.sbuf_tensor("ones_sb", [1, 512], BF16))
        rep_sb = stack.enter_context(nc.sbuf_tensor("rep_sb", [NE, 128], BF16))
        vs_all = stack.enter_context(nc.sbuf_tensor("vs_all", [128, SEQ * 16], F32))
        fc_sb = stack.enter_context(nc.sbuf_tensor("fc_sb", [128, 2], F32))
        gb_sb = stack.enter_context(nc.sbuf_tensor("gb_sb", [1, NE], BF16))
        bi2_sb = stack.enter_context(nc.sbuf_tensor("bi2_sb", [128, NFC], F32))
        bo_f = stack.enter_context(nc.sbuf_tensor("bo_f", [1, D], F32))
        bo2_sb = stack.enter_context(nc.sbuf_tensor("bo2_sb", [1, D], BF16))

        for dst, srcap in ((ident, ident_in), (maskA, maskA_in), (maskB, maskB_in),
                           (ones_row, ones_in), (rep_sb, rep_in)):
            em.emit("sync", lambda e, d=dst, s=srcap: e.dma_start(out=d[:, :], in_=s[:, :]), inc=sQC)
        em.emit("gpsimd", lambda e: e.dma_start(out=gb_sb[:, :], in_=gb_in[:, :]), inc=sQC2)
        em.emit("sync", lambda e: e.dma_start(out=bi2_sb[:, :], in_=bi2_in[:, :]), inc=sQC)
        em.emit("sync", lambda e: e.dma_start(out=bo_f[:, :], in_=bo_in[:, :]), inc=sQC)
        consts_tok = (sQC, sQC.n)
        consts_tok2 = (sQC2, sQC2.n)
        t_bo2 = em.emit("scalar", lambda e: e.activation(
            bo2_sb[:, :], bo_f[:, :], AF.Identity, scale=2.0), waits=[consts_tok], inc=sSE)
        em.emit("vector", lambda e: e.memset(fc_sb[:, 0:1], EXP_BIAS))
        t_fc = em.emit("vector", lambda e: e.memset(fc_sb[:, 1:2], 1e-5), inc=sVE)

        pa = contextlib.ExitStack()
        with pa:
            wq_sb = pa.enter_context(nc.sbuf_tensor("wq_sb", [128, 8 * D], BF16))
            wk_sb = pa.enter_context(nc.sbuf_tensor("wk_sb", [128, 8 * D], BF16))
            wv_sb = pa.enter_context(nc.sbuf_tensor("wv_sb", [128, 8 * D], BF16))
            wo_sb = pa.enter_context(nc.sbuf_tensor("wo_sb", [128, 8 * D], BF16))
            gw_sb = pa.enter_context(nc.sbuf_tensor("gw_sb", [128, 8 * NE], BF16))
            aT_sb = pa.enter_context(nc.sbuf_tensor("aT_sb", [128, 8 * 128], BF16))

            x_blk = [pa.enter_context(nc.sbuf_tensor(f"x_blk{i}", [128, 4 * D], F32)) for i in range(2)]
            xn_o = pa.enter_context(nc.sbuf_tensor("xn_o", [128, 4 * D], BF16))
            o_sb = pa.enter_context(nc.sbuf_tensor("o_sb", [128, 4 * D], BF16))
            lnst = pa.enter_context(nc.sbuf_tensor("lnst", [128, 32], F32))
            xnT_blk = pa.enter_context(nc.sbuf_tensor("xnT_blk", [128, 8 * W], BF16))
            qoT_blk = pa.enter_context(nc.sbuf_tensor("qoT_blk", [128, 8 * W], BF16))
            kT_blk = [pa.enter_context(nc.sbuf_tensor(f"kT_blk{i}", [128, 8 * W], BF16)) for i in range(2)]
            vaug = [pa.enter_context(nc.sbuf_tensor(f"vaug{i}", [128, 4 * H * 65], BF16)) for i in range(2)]
            expT = [pa.enter_context(nc.sbuf_tensor(f"expT{i}", [128, 8 * W], BF16)) for i in range(3)]
            attnp_ev = [pa.enter_context(nc.sbuf_tensor(f"attnp_ev{i}", [128, D], F32)) for i in range(2)]
            gt = pa.enter_context(nc.sbuf_tensor("gt", [128, 2 * NE], F32))
            gs = pa.enter_context(nc.sbuf_tensor("gs", [128, 8], F32))
            m1m2 = pa.enter_context(nc.sbuf_tensor("m1m2", [128, 4 * 2 * NE], BF16))
            m1T_sb = pa.enter_context(nc.sbuf_tensor("m1T_sb", [NE, W], BF16))
            m2T_sb = pa.enter_context(nc.sbuf_tensor("m2T_sb", [NE, W], BF16))
            s_sb = pa.enter_context(nc.sbuf_tensor("s_sb", [128, W], BF16))
            s12_sb = pa.enter_context(nc.sbuf_tensor("s12_sb", [128, W], BF16))
            rcp = pa.enter_context(nc.sbuf_tensor("rcp", [128, 8], F32))

            ps_ch = [pa.enter_context(nc.psum_tensor(f"ps_ch{i}", [128, 512], F32)) for i in range(2)]
            ps_sc = [pa.enter_context(nc.psum_tensor(f"ps_sc{i}", [128, 512], F32)) for i in range(2)]
            ps_pv = [pa.enter_context(nc.psum_tensor(f"ps_pv{i}", [128, 512], F32)) for i in range(2)]
            ps_tr = [pa.enter_context(nc.psum_tensor(f"ps_tr{i}", [128, 1024], BF16)) for i in range(2)]

            B_x = [Buf(), Buf()]
            B_xn = Buf()
            B_o = Buf()
            B_xnT = Buf()
            B_qoT = Buf()
            B_kT = [Buf(), Buf()]
            B_vaug = [Buf(), Buf()]
            B_expT = [Buf(), Buf(), Buf()]
            B_attnp = [Buf(), Buf()]
            B_pch = [Buf(), Buf()]
            B_psc = [Buf(), Buf()]
            B_ppv = [Buf(), Buf()]
            B_ptr = [Buf(), Buf()]
            B_m1m2 = Buf()
            B_m1T = Buf()
            B_m2T = Buf()
            B_s = Buf()
            B_s12 = Buf()
            B_gs = Buf()

            for par in range(2):
                def setones(e, pv=par):
                    v = vaug[pv][:, :].rearrange("p (c x) -> p c x", x=65)
                    return e.memset(v[:, :, 64:65], 1.0)
                B_vaug[par].set_ready(em.emit("vector", setones, inc=sVE))

            wtok = {}
            for nm, dst, src in (("wq", wq_sb, wqT), ("wk", wk_sb, wkT),
                                 ("wv", wv_sb, wvT), ("wo", wo_sb, woT)):
                t = None
                for k4 in range(2):
                    t = em.emit("gpsimd", lambda e, d=dst, s=src, k=k4: e.dma_start(
                        out=d[:, k * 4 * D:(k + 1) * 4 * D].rearrange("p (c d) -> p c d", d=D),
                        in_=s[k * 512:(k + 1) * 512, :].rearrange("(c p) d -> p c d", p=128)), inc=sQG)
                wtok[nm] = t
            for kd in range(8):
                em.emit("gpsimd", lambda e, k=kd: e.dma_start(
                    out=gw_sb[:, k * NE:(k + 1) * NE], in_=gwT_in[k * 128:(k + 1) * 128, :]), inc=sQW)
            t_gw = (sQW, sQW.n)
            t_gb = consts_tok2
            for kd in range(8):
                t_aT = em.emit("gpsimd", lambda e, k=kd: e.dma_start(
                    out=aT_sb[:, k * 128:(k + 1) * 128], in_=aT_in[k * 128:(k + 1) * 128, :]), inc=sQG)
            wq_tok, wk_tok, wv_tok, wo_tok = wtok["wq"], wtok["wk"], wtok["wv"], wtok["wo"]

            state = {"t_xn": None, "tel": None, "m12": [None] * 4, "sblk": None}

            def emit_prologue_a1(s, n):
                bi = s * NBLK + n
                par = bi % 2
                st_t_xn = state["t_xn"]
                st_prev_tel = state["tel"]
                t_x = em.emit("sync", lambda e, ss=s, nn=n, p=par: e.dma_start(
                    out=x_blk[p][:, :].rearrange("p (c d) -> p c d", d=D),
                    in_=x_in[ss * T + nn * W: ss * T + (nn + 1) * W, :]
                        .rearrange("(c p) d -> p c d", p=128)),
                    waits=dedup(B_x[par].wr_waits()), inc=sQL)
                B_x[par].set_ready(t_x)

                xn_w = dedup(B_xn.wr_waits())
                for c in range(4):
                    L = 8 * c
                    xc = x_blk[par][:, c * D:(c + 1) * D]
                    em.emit("vector", lambda e, xc=xc, L=L: e.reduce_sum(
                        lnst[:, L:L + 1], xc, axis=AX.X),
                        waits=dedup([t_x, st_t_xn]))
                    t_sq = em.emit("scalar", lambda e, xc=xc, cc=c, L=L: e.activation(
                        xn_o[:, cc * D:(cc + 1) * D], xc, AF.Square, accum_out=lnst[:, L + 1:L + 2]),
                        waits=dedup([t_x, st_t_xn] + (xn_w if c == 0 else [])), inc=sSE)
                    em.emit("vector", lambda e, L=L: e.tensor_scalar_mul(
                        lnst[:, L + 2:L + 3], lnst[:, L:L + 1], 1.0 / D), dr=True)
                    em.emit("vector", lambda e, L=L: e.tensor_scalar(
                        lnst[:, L + 3:L + 4], lnst[:, L:L + 1], lnst[:, L:L + 1], 1.0 / (D * D),
                        ALU.mult, ALU.mult))
                    t_v = em.emit("vector", lambda e, L=L: e.tensor_scalar(
                        lnst[:, L + 4:L + 5], lnst[:, L + 1:L + 2], 1.0 / D, lnst[:, L + 3:L + 4],
                        ALU.mult, ALU.subtract),
                        waits=[t_sq], inc=sVE, dr=True)
                    em.emit("scalar", lambda e, L=L: e.activation(
                        lnst[:, L + 5:L + 6], lnst[:, L + 4:L + 5], AF.Ln, bias=fc_sb[:, 1:2]),
                        waits=dedup([t_v, t_fc]))
                    t_rs = em.emit("scalar", lambda e, L=L: e.activation(
                        lnst[:, L + 6:L + 7], lnst[:, L + 5:L + 6], AF.Exp, scale=-0.5), inc=sSE, dr=True)
                    t_nm = em.emit("vector", lambda e, L=L: e.tensor_scalar(
                        lnst[:, L + 7:L + 8], lnst[:, L + 2:L + 3], lnst[:, L + 6:L + 7], -1.0,
                        ALU.mult, ALU.mult),
                        waits=[t_rs], inc=sVE)
                    st_t_xn = em.emit("scalar", lambda e, xc=xc, cc=c, L=L: e.activation(
                        xn_o[:, cc * D:(cc + 1) * D], xc, AF.Identity,
                        bias=lnst[:, L + 7:L + 8], scale=lnst[:, L + 6:L + 7]),
                        waits=dedup([t_nm] + xn_w), inc=sSE, dr=True)
                B_xn.set_ready(st_t_xn)
                B_x[par].add_reader(st_t_xn)
                if dbg and bi == 0:
                    t_d1 = em.emit("scalar", lambda e: e.dma_start(out=dbgx[:, :], in_=x_blk[0][:, :]),
                                   waits=[st_t_xn], inc=sQS)
                    t_d2 = em.emit("scalar", lambda e: e.dma_start(out=dbgxn[:, :], in_=xn_o[:, :]),
                                   waits=[st_t_xn], inc=sQS)
                    B_x[0].add_reader(t_d1)
                    B_xn.add_reader(t_d2)

                state["t_xn"] = st_t_xn
                state["tel"] = st_prev_tel

            def emit_prologue_a2(s, n):
                bi = s * NBLK + n
                par = bi % 2
                st_t_xn = state["t_xn"]
                st_prev_tel = state["tel"]
                xnT_w = dedup(B_xnT.wr_waits())
                t_ev = None
                for c in range(4):
                    for kd in range(8):
                        i = (c * 8 + kd) % 2
                        tw = dedup([B_xn.ready[0], consts_tok] + B_ptr[i].wr_waits() +
                                   (xnT_w if (c, kd) == (0, 0) else []))
                        t_tr = em.emit("tensor", lambda e, cc=c, k=kd, ii=i: e.transpose(
                            ps_tr[ii][:, 0:128],
                            xn_o[:, cc * D + k * 128: cc * D + k * 128 + 128], ident[:, :]),
                            waits=tw, inc=sPE)
                        B_ptr[i].set_ready(t_tr)
                        t_ev = em.emit("vector", lambda e, cc=c, k=kd, ii=i: e.tensor_copy(
                            xnT_blk[:, k * W + cc * 128: k * W + cc * 128 + 128], ps_tr[ii][:, 0:128]),
                            waits=[t_tr], inc=sVE, war=((c, kd) != (3, 7)))
                        B_ptr[i].add_reader(t_ev)
                B_xnT.set_ready(t_ev)
                B_xn.add_reader(t_ev)

                t_st = em.emit("scalar", lambda e, ss=s, nn=n: e.dma_start(
                    out=xnT_scr[ss, :, :, nn * W:(nn + 1) * W].rearrange("k p t -> p k t"),
                    in_=xnT_blk[:, :].rearrange("p (k t) -> p k t", t=W)),
                    waits=[B_xnT.ready[0]], inc=sQS)
                B_xnT.add_reader(t_st)

                for c in range(4):
                    pc = 1
                    gw_w = dedup(B_pch[pc].wr_waits() + [B_xnT.ready[0], t_gw, t_gb, consts_tok])
                    for kd in range(8):
                        em.emit("tensor", lambda e, k=kd, cc=c, p=pc: e.matmul(
                            ps_ch[p][:, 0:NE],
                            xnT_blk[:, k * W + cc * 128: k * W + cc * 128 + 128],
                            gw_sb[:, k * NE:(k + 1) * NE],
                            start=(k == 0), stop=False), waits=gw_w if kd == 0 else ())
                    t_g = em.emit("tensor", lambda e, p=pc: e.matmul(
                        ps_ch[p][:, 0:NE], ones_row[0:1, 0:128], gb_sb[0:1, :],
                        start=False, stop=True), inc=sPE)
                    B_pch[pc].set_ready(t_g)
                    gsw = dedup([t_g, st_prev_tel] + B_gs.wr_waits())
                    em.emit("vector", lambda e, p=pc: e.reduce_max(
                        gs[:, 0:1], ps_ch[p][:, 0:NE], axis=AX.X), waits=gsw)
                    em.emit("vector", lambda e: e.tensor_scalar_mul(gs[:, 1:2], gs[:, 0:1], -1.0), dr=True)
                    t_m1 = em.emit("vector", lambda e, p=pc, cc=c: e.tensor_scalar(
                        m1m2[:, cc * 2 * NE: cc * 2 * NE + NE], ps_ch[p][:, 0:NE], gs[:, 0:1], None, ALU.is_ge),
                        waits=dedup(B_m1m2.wr_waits() if c == 0 else []), inc=sVE)
                    t_el = em.emit("scalar", lambda e, p=pc: e.activation(
                        gt[:, 0:NE], ps_ch[p][:, 0:NE], AF.Exp, bias=gs[:, 1:2]),
                        waits=[t_m1], inc=sSE)
                    st_prev_tel = t_el
                    B_pch[pc].add_reader(t_el)
                    em.emit("vector", lambda e: e.reduce_sum(gs[:, 2:3], gt[:, 0:NE], axis=AX.X),
                            waits=[t_el])
                    em.emit("vector", lambda e, cc=c: e.scalar_tensor_tensor(
                        gt[:, NE:2 * NE], m1m2[:, cc * 2 * NE: cc * 2 * NE + NE], -2.0, gt[:, 0:NE], ALU.mult, ALU.add))
                    em.emit("vector", lambda e: e.reduce_max(gs[:, 4:5], gt[:, NE:2 * NE], axis=AX.X), dr=True)
                    t_m2 = em.emit("vector", lambda e, cc=c: e.tensor_scalar(
                        m1m2[:, cc * 2 * NE + NE: (cc + 1) * 2 * NE], gt[:, NE:2 * NE], gs[:, 4:5], None, ALU.is_ge), inc=sVE, dr=True)
                    em.emit("vector", lambda e: e.reciprocal(gs[:, 3:4], gs[:, 2:3]))
                    t_vs = em.emit("vector", lambda e, col=bi * 4 + c: e.tensor_scalar(
                        vs_all[:, col:col + 1], gs[:, 4:5], 1.0, gs[:, 3:4], ALU.add, ALU.mult), inc=sVE, dr=True)
                    B_gs.set_ready(t_vs)
                    state["m12"][c] = (t_m1, t_m2)
                    B_m1m2.set_ready(t_m1)

                state["t_xn"] = st_t_xn
                state["tel"] = st_prev_tel


            def emit_prologue_b(s, n):
                bi = s * NBLK + n
                for c in range(4):
                    t_m1, t_m2 = state["m12"][c]
                    for j, (off, mdstT, mbuf) in enumerate(((0, m1T_sb, B_m1T), (NE, m2T_sb, B_m2T))):
                        pt = (c * 2 + j) % 2
                        tw = dedup([t_m1, t_m2, consts_tok] + B_ptr[pt].wr_waits() +
                                   (mbuf.wr_waits() if c == 0 else []))
                        t_tr = em.emit("tensor", lambda e, o=off, p=pt, cc=c: e.transpose(
                            ps_tr[p][0:NE, 0:128], m1m2[:, cc * 2 * NE + o: cc * 2 * NE + o + NE], ident[:, :]),
                            waits=tw, inc=sPE)
                        B_ptr[pt].set_ready(t_tr)
                        t_mev = em.emit("vector", lambda e, dT=mdstT, cc=c, p=pt: e.tensor_copy(
                            dT[:, cc * 128:(cc + 1) * 128], ps_tr[p][0:NE, 0:128]),
                            waits=[t_tr], inc=sVE)
                        B_ptr[pt].add_reader(t_mev)
                        if c == 3:
                            mbuf.set_ready(t_mev)
                        if j == 1:
                            B_m1m2.add_reader(t_tr)
                sw = dedup(B_pch[0].wr_waits() + [B_xnT.ready[0], t_aT])
                t_schain = None
                for kd in range(8):
                    t_schain = em.emit("tensor", lambda e, k=kd: e.matmul(
                        ps_ch[0][:, 0:W], aT_sb[:, k * 128:(k + 1) * 128], xnT_blk[:, k * W:(k + 1) * W],
                        start=(k == 0), stop=(k == 7)),
                        waits=sw if kd == 0 else (), inc=sPE if kd == 7 else None)
                B_pch[0].set_ready(t_schain)
                t_sev = em.emit("vector", lambda e: e.tensor_copy(s_sb[:, :], ps_ch[0][:, 0:W]),
                                waits=dedup([t_schain] + B_s.wr_waits()), inc=sVE)
                B_s.set_ready(t_sev)
                B_pch[0].add_reader(t_sev)
                for j, (mT, mbuf, scr) in enumerate(
                        ((m1T_sb, B_m1T, s1_scr), (m2T_sb, B_m2T, s2_scr))):
                    dst = s12_sb if j == 0 else s_sb
                    dbuf2 = B_s12 if j == 0 else B_s
                    rw = dedup(B_pch[1].wr_waits() + mbuf.ready + [consts_tok])
                    t_rep = em.emit("tensor", lambda e, mm=mT: e.matmul(
                        ps_ch[1][:, 0:W], rep_sb[:, :], mm[:, :], start=True, stop=True),
                        waits=rw, inc=sPE)
                    B_pch[1].set_ready(t_rep)
                    mbuf.add_reader(t_rep)
                    t_s12 = em.emit("vector", lambda e, d=dst: e.tensor_tensor(
                        d[:, 0:W], s_sb[:, :], ps_ch[1][:, 0:W], ALU.mult),
                        waits=dedup([t_rep, t_sev] + (B_s12.wr_waits() if j == 0 else [])), inc=sVE)
                    B_pch[1].add_reader(t_s12)
                    B_s.add_reader(t_s12)
                    dbuf2.set_ready(t_s12)
                    t_st = em.emit("scalar", lambda e, d=dst, sc=scr, ss=s, nn=n: e.dma_start(
                        out=sc[:, ss * T + nn * W: ss * T + (nn + 1) * W],
                        in_=d[:, 0:W]), waits=[t_s12], inc=sQS)
                    dbuf2.add_reader(t_st)




            kT_toks = [[None] * 8, [None] * 8]

            emit_prologue_a1(0, 0)
            emit_prologue_a2(0, 0)
            emit_prologue_b(0, 0)
            for s in range(SEQ):
                for n in range(NBLK):
                    bi = s * NBLK + n
                    par = bi % 2
                    bpar = n % 2

                    qT_toks = [None] * 8
                    kT_toks_cur = [None] * 8
                    for m in range(8):
                        for which, (wsb, wtk, scale) in enumerate(
                                ((wq_sb, wq_tok, 0.125), (wk_sb, wk_tok, None))):
                            dstT = qoT_blk if which == 0 else kT_blk[bpar]
                            dbuf = B_qoT if which == 0 else B_kT[bpar]
                            if m == 0:
                                if which == 0:
                                    qoT_w0 = dedup(dbuf.wr_waits())
                                else:
                                    kT_w0 = dedup(dbuf.wr_waits())
                            dw = qoT_w0 if which == 0 else kT_w0
                            pc = (m * 2 + which) % 2
                            mw = dedup(B_pch[pc].wr_waits() + [B_xnT.ready[0], wtk])
                            t_mm = None
                            for kd in range(8):
                                t_mm = em.emit("tensor", lambda e, k=kd, mm=m, p=pc, ws=wsb: e.matmul(
                                    ps_ch[p][:, 0:W], ws[:, k * D + mm * 128: k * D + mm * 128 + 128],
                                    xnT_blk[:, k * W:(k + 1) * W],
                                    start=(k == 0), stop=(k == 7)),
                                    waits=mw if kd == 0 else (), inc=sPE if kd == 7 else None)
                            B_pch[pc].set_ready(t_mm)
                            if scale is not None:
                                t_ev = em.emit("vector", lambda e, mm=m, p=pc, dT=dstT: e.tensor_scalar_mul(
                                    dT[:, mm * W:(mm + 1) * W], ps_ch[p][:, 0:W], 0.125),
                                    waits=dedup([t_mm] + (dw if m == 0 else [])), inc=sVE)
                                qT_toks[m] = t_ev
                            else:
                                t_ev = em.emit("vector", lambda e, mm=m, p=pc, dT=dstT: e.tensor_copy(
                                    dT[:, mm * W:(mm + 1) * W], ps_ch[p][:, 0:W]),
                                    waits=dedup([t_mm] + (dw if m == 0 else [])), inc=sVE)
                                kT_toks_cur[m] = t_ev
                            B_pch[pc].add_reader(t_ev)
                    B_qoT.set_ready(qT_toks[7])
                    B_kT[bpar].set_ready(kT_toks_cur[7])
                    kT_toks[bpar] = kT_toks_cur
                    vg_w = dedup(B_vaug[bpar].wr_waits())
                    t_vev = None
                    for c in range(4):
                        for hf in range(2):
                            pc = (c * 2 + hf) % 2
                            mw = dedup(B_pch[pc].wr_waits() + [B_xnT.ready[0], wv_tok])
                            t_mm = None
                            for kd in range(8):
                                t_mm = em.emit("tensor", lambda e, k=kd, cc=c, h=hf, p=pc: e.matmul(
                                    ps_ch[p][:, 0:W],
                                    xnT_blk[:, k * W + cc * 128: k * W + cc * 128 + 128],
                                    wv_sb[:, k * D + h * W: k * D + (h + 1) * W],
                                    start=(k == 0), stop=(k == 7)),
                                    waits=mw if kd == 0 else (), inc=sPE if kd == 7 else None)
                            B_pch[pc].set_ready(t_mm)

                            def vev(e, cc=c, h=hf, p=pc, bp=bpar):
                                base = cc * H * 65 + h * 8 * 65
                                dst = vaug[bp][:, base: base + 8 * 65]
                                dstv = dst.rearrange("p (hh x) -> p hh x", x=65)[:, :, 0:64]
                                srcv = ps_ch[p][:, 0:W].rearrange("p (hh x) -> p hh x", x=64)
                                return e.tensor_copy(dstv, srcv)
                            t_vev = em.emit("vector", vev,
                                            waits=dedup([t_mm] + (vg_w if (c, hf) == (0, 0) else [])),
                                            inc=sVE, war=((c, hf) != (3, 1)))
                            B_pch[pc].add_reader(t_vev)
                    B_vaug[bpar].set_ready(t_vev)
                    B_xnT.add_reader(t_vev)
                    if bi > 0:
                        emit_prologue_b(s, n)

                    sc_pool = [ps_sc[0], ps_sc[1], ps_ch[0], ps_ch[1]]
                    sc_bufs = [B_psc[0], B_psc[1], B_pch[0], B_pch[1]]
                    o_w = dedup(B_o.wr_waits())
                    rcp_w = [[] for _ in range(8)]
                    msk_tok = [None, None, None]
                    last_o = [None]

                    def emit_scores(h, pv_h=None, n=n, bpar=bpar):
                        hp = h % 3
                        rr = sc_ranges(n)
                        pace = max(1, len(rr) // 4)
                        hb = (h % 2) * 64
                        ck = h // 2
                        ew = dedup(B_expT[hp].wr_waits())
                        t_exp = None
                        t_msk = None
                        for ri, (kind, jc, i0, wdt) in enumerate(rr):
                            slot = jc if kind == "prev" else 4 + jc
                            kbuf = kT_blk[(n - 1) % 2] if kind == "prev" else kT_blk[bpar]
                            ck2 = h // 2
                            if kind == "prev":
                                ktoks = [kT_toks[(n - 1) % 2][ck2]]
                            else:
                                ktoks = [kT_toks[bpar][ck2]]
                            psc = ri % 4
                            pbank = sc_pool[psc]
                            pbuf = sc_bufs[psc]
                            mw = dedup(pbuf.wr_waits() + ktoks + [qT_toks[ck2]])
                            t_mm = em.emit("tensor", lambda e, kb=kbuf, b=hb, c=ck, j=jc, ii=i0, wd=wdt, pb=pbank: e.matmul(
                                pb[:, ii:ii + wd],
                                kb[b:b + 64, c * W + j * 128: c * W + j * 128 + 128],
                                qoT_blk[b:b + 64, c * W + ii: c * W + ii + wd],
                                start=True, stop=True),
                                waits=mw, inc=sPE)
                            pbuf.set_ready(t_mm)
                            first = (kind, jc) == (("own", 0) if n == 0 else ("prev", 0))
                            t_exp = em.emit("scalar", lambda e, sl=slot, ii=i0, wd=wdt, pb=pbank, hh=hp: e.activation(
                                expT[hh][:, sl * W + ii: sl * W + ii + wd],
                                pb[:, ii:ii + wd], AF.Exp, bias=fc_sb[:, 0:1]),
                                waits=dedup([t_mm, t_fc] + (ew if first else [])),
                                inc=sSE, war=(ri != len(rr) - 1))
                            pbuf.add_reader(t_exp)
                            if pv_h is not None and ri % pace == pace - 1 and ri // pace < 4:
                                emit_pv_ic(pv_h, ri // pace)
                        for mi, (kind, jc, i0, wdt) in enumerate(sc_ranges(n)):
                            slot = jc if kind == "prev" else 4 + jc
                            mk = maskA if kind == "prev" else maskB
                            dcol = slot * W + jc * 128
                            t_msk = em.emit("gpsimd", lambda e, dc=dcol, m=mk, hh=hp: e.tensor_mul(
                                expT[hh][:, dc:dc + 128], expT[hh][:, dc:dc + 128], m[:, :]),
                                waits=dedup([t_exp, consts_tok]),
                                inc=sGP if mi == len(sc_ranges(n)) - 1 else None)
                        B_expT[hp].set_ready(t_msk)
                        return t_msk

                    def emit_pv_ic(h, ic, n=n, bpar=bpar, last=False):
                        hp = h % 3
                        t_msk = msk_tok[hp]
                        if True:
                            tiles = ([("prev", jc) for jc in range(ic, 4)] if n > 0 else []) + \
                                    [("own", jc) for jc in range(0, ic + 1)]
                            pp = ic % 2
                            mw = dedup(B_ppv[pp].wr_waits() + [t_msk] +
                                       B_vaug[bpar].ready + (B_vaug[(n - 1) % 2].ready if n > 0 else []))
                            t_pv = None
                            for ti, (kind, jc) in enumerate(tiles):
                                slot = jc if kind == "prev" else 4 + jc
                                vb = vaug[(n - 1) % 2] if kind == "prev" else vaug[bpar]
                                t_pv = em.emit("tensor", lambda e, sl=slot, j=jc, icc=ic, v=vb, hh=hp, p=pp, hd=h,
                                               st=(ti == 0), sp=(ti == len(tiles) - 1): e.matmul(
                                    ps_pv[p][:, 0:65],
                                    expT[hh][:, sl * W + icc * 128: sl * W + icc * 128 + 128],
                                    v[:, j * H * 65 + hd * 65: j * H * 65 + hd * 65 + 65],
                                    start=st, stop=sp),
                                    waits=mw if ti == 0 else (),
                                    inc=sPE if ti == len(tiles) - 1 else None)
                            B_ppv[pp].set_ready(t_pv)
                            B_vaug[bpar].add_reader(t_pv)
                            if n > 0:
                                B_vaug[(n - 1) % 2].add_reader(t_pv)
                            rc = (h * 4 + ic) % 8
                            em.emit("vector", lambda e, p=pp, r=rc: e.reciprocal(
                                rcp[:, r:r + 1], ps_pv[p][:, 64:65]),
                                waits=dedup([t_pv] + (o_w if (h, ic) == (0, 0) else [])))
                            t_o = em.emit("vector", lambda e, icc=ic, hd=h, p=pp, r=rc: e.tensor_scalar(
                                o_sb[:, icc * D + hd * 64: icc * D + hd * 64 + 64],
                                ps_pv[p][:, 0:64], rcp[:, r:r + 1], None, ALU.mult),
                                inc=sVE, dr=True, war=(not (last and ic == 3)))
                            B_ppv[pp].add_reader(t_o)
                            B_expT[hp].add_reader(t_o)
                            last_o[0] = t_o

                    def emit_pv(h, last=False):
                        for ic in range(4):
                            emit_pv_ic(h, ic, last=last)

                    for h in range(H):
                        msk_tok[h % 3] = emit_scores(h, pv_h=(h - 2 if h >= 2 else None))
                        if h == 2 and bi + 1 < SEQ * NBLK:
                            emit_prologue_a1((bi + 1) // NBLK, (bi + 1) % NBLK)
                        if h == 12 and bi + 1 < SEQ * NBLK:
                            emit_prologue_a2((bi + 1) // NBLK, (bi + 1) % NBLK)
                    emit_pv(H - 2)
                    emit_pv(H - 1, last=True)
                    t_o = last_o[0]
                    B_o.set_ready(t_o)
                    t_attn = (sPE, sPE.n)
                    B_qoT.add_reader(t_attn)
                    B_kT[0].add_reader(t_attn)
                    B_kT[1].add_reader(t_attn)

                    qw = dedup(B_qoT.wr_waits())
                    t_ev = None
                    for c in range(4):
                        for kd in range(8):
                            i = (c * 8 + kd) % 2
                            tw = dedup([B_o.ready[0], consts_tok] + B_ptr[i].wr_waits() +
                                       (qw if (c, kd) == (0, 0) else []))
                            t_tr = em.emit("tensor", lambda e, cc=c, k=kd, ii=i: e.transpose(
                                ps_tr[ii][:, 0:128],
                                o_sb[:, cc * D + k * 128: cc * D + k * 128 + 128], ident[:, :]),
                                waits=tw, inc=sPE)
                            B_ptr[i].set_ready(t_tr)
                            t_ev = em.emit("vector", lambda e, cc=c, k=kd, ii=i: e.tensor_copy(
                                qoT_blk[:, k * W + cc * 128: k * W + cc * 128 + 128], ps_tr[ii][:, 0:128]),
                                waits=[t_tr], inc=sVE, war=((c, kd) != (3, 7)))
                            B_ptr[i].add_reader(t_ev)
                    B_qoT.set_ready(t_ev)
                    B_o.add_reader(t_ev)

                    for c in range(4):
                        ap_par = c % 2
                        apw = dedup(B_attnp[ap_par].wr_waits())
                        t_ap = None
                        for hf in range(2):
                            pc = (c * 2 + hf) % 2
                            mw = dedup(B_pch[pc].wr_waits() + B_qoT.ready + [wo_tok])
                            t_mm = None
                            for kd in range(8):
                                t_mm = em.emit("tensor", lambda e, k=kd, cc=c, h=hf, p=pc: e.matmul(
                                    ps_ch[p][:, 0:W],
                                    qoT_blk[:, k * W + cc * 128: k * W + cc * 128 + 128],
                                    wo_sb[:, k * D + h * W: k * D + (h + 1) * W],
                                    start=(k == 0), stop=(k == 7)),
                                    waits=mw if kd == 0 else (), inc=sPE if kd == 7 else None)
                            B_pch[pc].set_ready(t_mm)
                            B_qoT.add_reader(t_mm)
                            t_ap = em.emit("vector", lambda e, cc=c, h=hf, p=pc, ap=ap_par, px=par: e.tensor_tensor(
                                attnp_ev[ap][:, h * W:(h + 1) * W], ps_ch[p][:, 0:W],
                                x_blk[px][:, cc * D + h * W: cc * D + h * W + W], ALU.add),
                                waits=dedup([t_mm] + B_x[par].ready + (apw if hf == 0 else [])), inc=sVE)
                            B_pch[pc].add_reader(t_ap)
                            B_x[par].add_reader(t_ap)
                        B_attnp[ap_par].set_ready(t_ap)
                        t_st = em.emit("scalar", lambda e, cc=c, ss=s, nn=n, ap=ap_par: e.dma_start(
                            out=attnp_scr[ss * T + nn * W + cc * 128: ss * T + nn * W + (cc + 1) * 128, :],
                            in_=attnp_ev[ap][:, :]), waits=[t_ap], inc=sQS)
                        B_attnp[ap_par].add_reader(t_st)

            feA = [(sPE, sPE.n), (sVE, sVE.n), (sSE, sSE.n), (sQS, sQS.n), (sQL, sQL.n)]

        pb = contextlib.ExitStack()
        with pb:
            wi_sb = pb.enter_context(nc.sbuf_tensor("wi_sb", [128, 8 * DFF], BF16))
            wio_sb = pb.enter_context(nc.sbuf_tensor("wio_sb", [128, NFC * D], BF16))
            bst_sb = pb.enter_context(nc.sbuf_tensor("bst_sb", [128, DFF], BF16))
            xnT_f = [pb.enter_context(nc.sbuf_tensor(f"xnT_f{i}", [128, 8 * TT], BF16)) for i in range(2)]
            s1f = [pb.enter_context(nc.sbuf_tensor(f"s1f{i}", [128, TT], BF16)) for i in range(2)]
            s2f = [pb.enter_context(nc.sbuf_tensor(f"s2f{i}", [128, TT], BF16)) for i in range(2)]
            hsum = pb.enter_context(nc.sbuf_tensor("hsum", [128, NFC * TT], BF16))
            attnp_f = [pb.enter_context(nc.sbuf_tensor(f"attnp_f{i}", [128, 2 * D], F32)) for i in range(2)]

            ps_z = [pb.enter_context(nc.psum_tensor(f"ps_z{i}", [128, 512], F32)) for i in range(4)]
            ps_wo = [pb.enter_context(nc.psum_tensor(f"ps_wo{i}", [128, 512], F32)) for i in range(4)]

            t_wi = t_wo2 = None
            for k4 in range(2):
                t_wi = em.emit("gpsimd", lambda e, k=k4: e.dma_start(
                    out=wi_sb[:, k * 4 * DFF:(k + 1) * 4 * DFF].rearrange("p (c d) -> p c d", d=DFF),
                    in_=wiT_in[k * 512:(k + 1) * 512, :].rearrange("(c p) d -> p c d", p=128)),
                    waits=feA if k4 == 0 else (), inc=sQG)
            for fk in range(8):
                t_wo2 = em.emit("gpsimd", lambda e, k=fk: e.dma_start(
                    out=wio_sb[:, k * 4 * D:(k + 1) * 4 * D].rearrange("p (c d) -> p c d", d=D),
                    in_=wioT_in[k * 512:(k + 1) * 512, :].rearrange("(c p) d -> p c d", p=128)), inc=sQG)
            t_bst = em.emit("gpsimd", lambda e: e.dma_start(out=bst_sb[:, :], in_=bst_in[:, :]), inc=sQG)
            t_bi = consts_tok
            h2_sb = [pb.enter_context(nc.sbuf_tensor(f"h2_sb{i}", [128, TT], BF16)) for i in range(2)]
            base_bf = [pb.enter_context(nc.sbuf_tensor(f"base_bf{i}", [128, TT], BF16)) for i in range(2)]
            B_h2 = [Buf(), Buf()]
            B_base = [Buf(), Buf()]

            B_xf = [Buf(), Buf()]
            B_s1 = [Buf(), Buf()]
            B_s2 = [Buf(), Buf()]
            B_hs = Buf()
            B_af = [Buf(), Buf()]
            B_z = [Buf() for _ in range(4)]
            B_w = [Buf() for _ in range(4)]

            for tl in range(NTILE):
                s = tl // (T // TT)
                t0 = (tl % (T // TT)) * TT
                par = tl % 2

                t_lx = em.emit("sync", lambda e, ss=s, tt=t0, p=par: e.dma_start(
                    out=xnT_f[p][:, :].rearrange("p (k t) -> p k t", t=TT),
                    in_=xnT_scr[ss, :, :, tt:tt + TT].rearrange("k p t -> p k t")),
                    waits=dedup(B_xf[par].wr_waits() + (feA if tl == 0 else [])), inc=sQL)
                B_xf[par].set_ready(t_lx)
                t_l1 = em.emit("sync", lambda e, ss=s, tt=t0, p=par: e.dma_start(
                    out=s1f[p][:, :], in_=s1_scr[:, ss * T + tt: ss * T + tt + TT]),
                    waits=dedup(B_s1[par].wr_waits() + (feA if tl == 0 else [])), inc=sQL)
                B_s1[par].set_ready(t_l1)
                t_l2 = em.emit("sync", lambda e, ss=s, tt=t0, p=par: e.dma_start(
                    out=s2f[p][:, :], in_=s2_scr[:, ss * T + tt: ss * T + tt + TT]),
                    waits=dedup(B_s2[par].wr_waits() + (feA if tl == 0 else [])), inc=sQL)
                B_s2[par].set_ready(t_l2)
                t_la = em.emit("sync", lambda e, ss=s, tt=t0, p=par: e.dma_start(
                    out=attnp_f[p][:, :].rearrange("p (c d) -> p c d", d=D),
                    in_=attnp_scr[ss * T + tt: ss * T + tt + TT, :].rearrange("(c p) d -> p c d", p=128)),
                    waits=dedup(B_af[par].wr_waits() + (feA if tl == 0 else [])), inc=sQL)
                B_af[par].set_ready(t_la)

                hs_w = dedup(B_hs.wr_waits())
                t_h2 = None

                def emit_base(fc):
                    fp = fc % 2
                    bb = fp * 3
                    mw = dedup(B_z[bb].wr_waits() + [t_lx, t_wi])
                    for kd in range(8):
                        em.emit("tensor", lambda e, k=kd, f=fc, z=bb, p=par: e.matmul(
                            ps_z[z][:, 0:TT],
                            wi_sb[:, k * DFF + f * 128: k * DFF + f * 128 + 128],
                            xnT_f[p][:, k * TT:(k + 1) * TT],
                            start=(k == 0), stop=(k == 7)),
                            waits=mw if kd == 0 else (), inc=sPE if kd == 7 else None)
                    t_bmm = (sPE, sPE.n)
                    B_z[bb].set_ready(t_bmm)
                    t_bev = em.emit("vector", lambda e, z=bb, p=fp: e.tensor_copy(
                        base_bf[p][:, :], ps_z[z][:, 0:TT]),
                        waits=dedup([t_bmm] + B_base[fp].wr_waits()), inc=sVE)
                    B_z[bb].add_reader(t_bev)
                    B_base[fp].set_ready(t_bev)

                def emit_z(fc):
                    fp = fc % 2
                    for which, (sfb, stok, lb) in enumerate(((s1f[par], t_l1, 1), (s2f[par], t_l2, 2))):
                        em.emit("tensor", lambda e, p=fp, z=lb: e.matmul(
                            ps_z[z][:, 0:TT], ident[:, :], base_bf[p][:, :], start=True, stop=False),
                            waits=dedup(B_z[lb].wr_waits() + B_base[fp].ready + [consts_tok]))
                        t_z = em.emit("tensor", lambda e, f=fc, z=lb, sf=sfb: e.matmul(
                            ps_z[z][:, 0:TT], bst_sb[:, f * 128:(f + 1) * 128], sf[:, :],
                            start=False, stop=True),
                            waits=dedup([stok, t_bst]), inc=sPE)
                        B_z[lb].set_ready(t_z)
                        if which == 1:
                            B_base[fp].add_reader(t_z)
                    t_h1 = em.emit("scalar", lambda e, f=fc: e.activation(
                        hsum[:, f * TT:(f + 1) * TT], ps_z[1][:, 0:TT], AF.Relu,
                        bias=bi2_sb[:, f:f + 1]),
                        waits=dedup([B_z[1].ready[0], t_bi] + (hs_w if fc == 0 else [])))
                    t_hb = em.emit("scalar", lambda e, f=fc, p=fp: e.activation(
                        h2_sb[p][:, :], ps_z[2][:, 0:TT], AF.Relu, bias=bi2_sb[:, f:f + 1]),
                        waits=dedup([B_z[2].ready[0]] + B_h2[fp].wr_waits()), inc=sSE)
                    B_z[1].add_reader(t_hb)
                    B_z[2].add_reader(t_hb)
                    B_h2[fp].set_ready(t_hb)
                    t = em.emit("vector", lambda e, f=fc, p=fp: e.tensor_tensor(
                        hsum[:, f * TT:(f + 1) * TT], hsum[:, f * TT:(f + 1) * TT],
                        h2_sb[p][:, :], ALU.add),
                        waits=dedup([t_hb]), inc=sVE, war=(fc != NFC - 1))
                    B_h2[fp].add_reader(t)
                    return t

                h2_toks = [None] * NFC
                for fc in range(NFC + 1):
                    if fc < NFC:
                        emit_base(fc)
                    if fc >= 1:
                        t_h2 = emit_z(fc - 1)
                        h2_toks[fc - 1] = t_h2
                B_hs.set_ready(t_h2)
                B_xf[par].add_reader((sPE, sPE.n))
                B_s1[par].add_reader((sPE, sPE.n))
                B_s2[par].add_reader((sPE, sPE.n))

                t_wo_done = None
                for tsub in range(2):
                    for dh in range(2):
                        wb = tsub * 2 + dh
                        mw = dedup([t_wo2, t_bo2] + B_w[wb].wr_waits())
                        for fc in range(NFC):
                            em.emit("tensor", lambda e, f=fc, ts=tsub, d=dh, b=wb: e.matmul(
                                ps_wo[b][:, 0:512],
                                hsum[:, f * TT + ts * 128: f * TT + ts * 128 + 128],
                                wio_sb[:, f * D + d * 512: f * D + (d + 1) * 512],
                                start=(f == 0), stop=False),
                                waits=dedup((mw if fc == 0 else []) + [h2_toks[min(fc | 3, NFC - 1)]]))
                        t_wo_done = em.emit("tensor", lambda e, d=dh, b=wb: e.matmul(
                            ps_wo[b][:, 0:512], ones_row[0:1, 0:128], bo2_sb[0:1, d * 512:(d + 1) * 512],
                            start=False, stop=True), inc=sPE)
                        B_w[wb].set_ready(t_wo_done)
                B_hs.add_reader(t_wo_done)

                t_ev = None
                for tsub in range(2):
                    col = s * 16 + (t0 // 128) + tsub
                    for dh in range(2):
                        wb = tsub * 2 + dh
                        t_ev = em.emit("vector", lambda e, ts=tsub, d=dh, b=wb, p=par, cc=col: e.scalar_tensor_tensor(
                            attnp_f[p][:, ts * D + d * 512: ts * D + (d + 1) * 512],
                            ps_wo[b][:, 0:512], vs_all[:, cc:cc + 1],
                            attnp_f[p][:, ts * D + d * 512: ts * D + (d + 1) * 512],
                            ALU.mult, ALU.add),
                            waits=dedup(B_w[wb].ready + [t_la]), inc=sVE)
                        B_w[wb].add_reader(t_ev)
                B_af[par].add_reader(t_ev)
                t_st = em.emit("scalar", lambda e, ss=s, tt=t0, p=par: e.dma_start(
                    out=out_hbm[ss * T + tt: ss * T + tt + TT, :].rearrange("(c p) d -> p c d", p=128),
                    in_=attnp_f[p][:, :].rearrange("p (c d) -> p c d", d=D)),
                    waits=[t_ev], inc=sQS)
                B_af[par].add_reader(t_st)

            final_store = (sQS, sQS.n)

        em.emit("sync", lambda e: e.nop(), waits=[final_store])

        with nc.Block() as block:
            @block.sync
            def _(sync):
                for op in em.ops["sync"]:
                    op(sync)

            @block.gpsimd
            def _(gpsimd):
                for op in em.ops["gpsimd"]:
                    op(gpsimd)

            @block.tensor
            def _(tensor):
                for op in em.ops["tensor"]:
                    op(tensor)

            @block.vector
            def _(vector):
                for op in em.ops["vector"]:
                    op(vector)

            @block.scalar
            def _(scalar):
                for op in em.ops["scalar"]:
                    op(scalar)

    return nc


_BF = ml_dtypes.bfloat16


def _consts():
    ident = np.eye(128, dtype=_BF)
    jl = np.arange(128)[:, None]
    il = np.arange(128)[None, :]
    maskA = (jl > il).astype(_BF)
    maskB = (jl <= il).astype(_BF)
    ones_row = np.ones((1, 512), dtype=_BF)
    rep = np.kron(np.eye(NE, dtype=np.float32), np.ones((1, RK), dtype=np.float32)).astype(_BF)
    return ident, maskA, maskB, ones_row, rep


def kernel(L0_attn_x, L0_attn_Wq, L0_attn_Wk, L0_attn_Wv, L0_attn_Wo,
           L0_ffn_wi, L0_ffn_bi, L0_ffn_wo, L0_ffn_bo,
           L0_ffn_lora_A, L0_ffn_lora_B, L0_ffn_gate_w, L0_ffn_gate_b):
    x = np.asarray(L0_attn_x, dtype=np.float32)
    shared = dict(
        wqT=np.ascontiguousarray(np.asarray(L0_attn_Wq, np.float32).T),
        wkT=np.ascontiguousarray(np.asarray(L0_attn_Wk, np.float32).T),
        wvT=np.ascontiguousarray(np.asarray(L0_attn_Wv, np.float32).T),
        woT=np.ascontiguousarray(np.asarray(L0_attn_Wo, np.float32).T),
        wiT=np.ascontiguousarray(np.asarray(L0_ffn_wi, np.float32).T),
        wioT=np.ascontiguousarray(np.asarray(L0_ffn_wo, np.float32).T),
        bi2=np.ascontiguousarray(np.asarray(L0_ffn_bi, np.float32).reshape(NFC, 128).T),
        bo=np.asarray(L0_ffn_bo, np.float32).reshape(1, D),
        aT=np.ascontiguousarray(np.asarray(L0_ffn_lora_A, np.float32).transpose(2, 0, 1).reshape(D, NE * RK)),
        bst=np.ascontiguousarray(np.asarray(L0_ffn_lora_B, np.float32).transpose(0, 2, 1).reshape(NE * RK, DFF)),
        gwT=np.ascontiguousarray(np.asarray(L0_ffn_gate_w, np.float32).T),
        gb=np.asarray(L0_ffn_gate_b, np.float32).reshape(1, NE),
    )
    ident, maskA, maskB, ones_row, rep = _consts()
    shared.update(ident=ident, maskA=maskA, maskB=maskB, ones_row=ones_row, rep=rep)

    nc = build_program()
    in_maps = []
    for c in range(NCORES):
        m = dict(shared)
        m["x"] = np.ascontiguousarray(x[c * SEQ:(c + 1) * SEQ].reshape(SEQ * T, D))
        in_maps.append(m)
    res = run_bass_kernel_spmd(nc, in_maps, core_ids=list(range(NCORES)))
    out = np.empty((NCORES * SEQ, T, D), dtype=np.float32)
    for c in range(NCORES):
        out[c * SEQ:(c + 1) * SEQ] = np.asarray(res.results[c]["out"]).reshape(SEQ, T, D)
    return out




# version 23
# speedup vs baseline: 1.1119x; 1.0458x over previous
import contextlib
import numpy as np
import ml_dtypes
import concourse.bass as bass
import concourse.mybir as mybir
from concourse.bass_utils import run_bass_kernel_spmd

F32 = mybir.dt.float32
BF16 = mybir.dt.bfloat16
AF = mybir.ActivationFunctionType
ALU = mybir.AluOpType
AX = mybir.AxisListType

NCORES = 8
SEQ = 4
T = 2048
D = 1024
H = 16
DH = 64
W = 512
NBLK = 4
DFF = 4096
NE = 8
RK = 16
TT = 256
NFC = DFF // 128
NTILE = SEQ * T // TT
EXP_BIAS = -12.0

ENGS = ("sync", "gpsimd", "tensor", "vector", "scalar")


class Sem:
    def __init__(self, handle, step, eng):
        self.h = handle
        self.step = step
        self.eng = eng
        self.n = 0

    def tok(self):
        self.n += self.step
        return (self, self.n)


class Emitter:
    def __init__(self):
        self.ops = {e: [] for e in ENGS}
        self.satisfied = {e: {} for e in ENGS}

    def emit(self, eng, fn, waits=(), inc=None, dr=False, war=False):
        need = []
        sat = self.satisfied[eng]
        if inc is not None and inc.step == 16:
            waits = list(waits) + [(inc, inc.n)]
        for wt in waits:
            if wt is None:
                continue
            sem, val = wt
            if sem.eng == eng and sem.step == 1:
                continue
            if sat.get(id(sem), 0) >= val:
                continue
            need.append((sem, val))
            sat[id(sem)] = val
        tok = inc.tok() if inc is not None else None
        step = inc.step if inc is not None else 0
        drain = (inc is not None and inc.step == 1 and not war
                 and eng in ("vector", "scalar"))

        def run(e, _need=tuple(need), _tok=tok, _fn=fn, _step=step, _dr=drain, _pre=dr):
            for (sem, val) in _need:
                e.wait_ge(sem.h, val)
            if _pre:
                e.drain()
            ins = _fn(e)
            if _tok is not None:
                if _dr:
                    e.drain().then_inc(_tok[0].h, _step)
                else:
                    ins.then_inc(_tok[0].h, _step)

        self.ops[eng].append(run)
        return tok


class Buf:
    def __init__(self):
        self.ready = []
        self.readers = []

    def wr_waits(self):
        return list(self.ready) + list(self.readers)

    def set_ready(self, tok):
        self.ready = [tok] if tok is not None else []
        self.readers = []

    def add_reader(self, tok):
        if tok is not None:
            self.readers.append(tok)


def dedup(waits):
    best = {}
    for w in waits:
        if w is None:
            continue
        sem, val = w
        k = id(sem)
        if k not in best or best[k][1] < val:
            best[k] = (sem, val)
    return list(best.values())


def sc_ranges(n):
    rr = []
    if n > 0:
        for jc in range(4):
            rr.append(("prev", jc, 0, min(W, 128 * (jc + 1))))
    for jc in range(4):
        rr.append(("own", jc, 128 * jc, W - 128 * jc))
    return rr


def build_program(dbg=False):
    nc = bass.Bass("TRN2", target_bir_lowering=False)
    scr_kind = dict(kind="ExternalOutput") if dbg else {}

    x_in = nc.dram_tensor("x", [SEQ * T, D], F32, kind="ExternalInput").ap()
    wqT = nc.dram_tensor("wqT", [D, D], F32, kind="ExternalInput").ap()
    wkT = nc.dram_tensor("wkT", [D, D], F32, kind="ExternalInput").ap()
    wvT = nc.dram_tensor("wvT", [D, D], F32, kind="ExternalInput").ap()
    woT = nc.dram_tensor("woT", [D, D], F32, kind="ExternalInput").ap()
    wiT_in = nc.dram_tensor("wiT", [D, DFF], F32, kind="ExternalInput").ap()
    wioT_in = nc.dram_tensor("wioT", [DFF, D], F32, kind="ExternalInput").ap()
    bi2_in = nc.dram_tensor("bi2", [128, NFC], F32, kind="ExternalInput").ap()
    bo_in = nc.dram_tensor("bo", [1, D], F32, kind="ExternalInput").ap()
    aT_in = nc.dram_tensor("aT", [D, NE * RK], F32, kind="ExternalInput").ap()
    bst_in = nc.dram_tensor("bst", [NE * RK, DFF], F32, kind="ExternalInput").ap()
    gwT_in = nc.dram_tensor("gwT", [D, NE], F32, kind="ExternalInput").ap()
    gb_in = nc.dram_tensor("gb", [1, NE], F32, kind="ExternalInput").ap()
    ident_in = nc.dram_tensor("ident", [128, 128], BF16, kind="ExternalInput").ap()
    maskA_in = nc.dram_tensor("maskA", [128, 128], BF16, kind="ExternalInput").ap()
    maskB_in = nc.dram_tensor("maskB", [128, 128], BF16, kind="ExternalInput").ap()
    ones_in = nc.dram_tensor("ones_row", [1, 512], BF16, kind="ExternalInput").ap()
    rep_in = nc.dram_tensor("rep", [NE, 128], BF16, kind="ExternalInput").ap()

    out_hbm = nc.dram_tensor("out", [SEQ * T, D], F32, kind="ExternalOutput").ap()

    if dbg:
        dbgx = nc.dram_tensor("dbgx", [128, 4 * D], F32, kind="ExternalOutput").ap()
        dbgxn = nc.dram_tensor("dbgxn", [128, 4 * D], BF16, kind="ExternalOutput").ap()
    xnT_scr = nc.dram_tensor("xnT_scr", [SEQ, 8, 128, T], BF16, **scr_kind).ap()
    s1_scr = nc.dram_tensor("s1_scr", [128, SEQ * T], BF16, **scr_kind).ap()
    s2_scr = nc.dram_tensor("s2_scr", [128, SEQ * T], BF16, **scr_kind).ap()
    attnp_scr = nc.dram_tensor("attnp_scr", [SEQ * T, D], F32, **scr_kind).ap()

    em = Emitter()
    stack = contextlib.ExitStack()
    with stack:
        sQL = Sem(stack.enter_context(nc.semaphore("sQL")), 16, "sync")
        sQC = Sem(stack.enter_context(nc.semaphore("sQC")), 16, "sync")
        sQC2 = Sem(stack.enter_context(nc.semaphore("sQC2")), 16, "gpsimd")
        sQG = Sem(stack.enter_context(nc.semaphore("sQG")), 16, "gpsimd")
        sQW = Sem(stack.enter_context(nc.semaphore("sQW")), 16, "gpsimd")
        sQS = Sem(stack.enter_context(nc.semaphore("sQS")), 16, "scalar")
        sPE = Sem(stack.enter_context(nc.semaphore("sPE")), 1, "tensor")
        sGP = Sem(stack.enter_context(nc.semaphore("sGP")), 1, "gpsimd")
        sVE = Sem(stack.enter_context(nc.semaphore("sVE")), 1, "vector")
        sSE = Sem(stack.enter_context(nc.semaphore("sSE")), 1, "scalar")

        ident = stack.enter_context(nc.sbuf_tensor("ident_sb", [128, 128], BF16))
        maskA = stack.enter_context(nc.sbuf_tensor("maskA_sb", [128, 128], BF16))
        maskB = stack.enter_context(nc.sbuf_tensor("maskB_sb", [128, 128], BF16))
        ones_row = stack.enter_context(nc.sbuf_tensor("ones_sb", [1, 512], BF16))
        rep_sb = stack.enter_context(nc.sbuf_tensor("rep_sb", [NE, 128], BF16))
        vs_all = stack.enter_context(nc.sbuf_tensor("vs_all", [128, SEQ * 16], F32))
        fc_sb = stack.enter_context(nc.sbuf_tensor("fc_sb", [128, 2], F32))
        gb_sb = stack.enter_context(nc.sbuf_tensor("gb_sb", [1, NE], BF16))
        bi2_sb = stack.enter_context(nc.sbuf_tensor("bi2_sb", [128, NFC], F32))
        bo_f = stack.enter_context(nc.sbuf_tensor("bo_f", [1, D], F32))
        bo2_sb = stack.enter_context(nc.sbuf_tensor("bo2_sb", [1, D], BF16))

        for dst, srcap in ((ident, ident_in), (maskA, maskA_in), (maskB, maskB_in),
                           (ones_row, ones_in), (rep_sb, rep_in)):
            em.emit("sync", lambda e, d=dst, s=srcap: e.dma_start(out=d[:, :], in_=s[:, :]), inc=sQC)
        em.emit("gpsimd", lambda e: e.dma_start(out=gb_sb[:, :], in_=gb_in[:, :]), inc=sQC2)
        em.emit("sync", lambda e: e.dma_start(out=bi2_sb[:, :], in_=bi2_in[:, :]), inc=sQC)
        em.emit("sync", lambda e: e.dma_start(out=bo_f[:, :], in_=bo_in[:, :]), inc=sQC)
        consts_tok = (sQC, sQC.n)
        consts_tok2 = (sQC2, sQC2.n)
        t_bo2 = em.emit("scalar", lambda e: e.activation(
            bo2_sb[:, :], bo_f[:, :], AF.Identity, scale=2.0), waits=[consts_tok], inc=sSE)
        em.emit("vector", lambda e: e.memset(fc_sb[:, 0:1], EXP_BIAS))
        t_fc = em.emit("vector", lambda e: e.memset(fc_sb[:, 1:2], 1e-5), inc=sVE)

        pa = contextlib.ExitStack()
        with pa:
            wq_sb = pa.enter_context(nc.sbuf_tensor("wq_sb", [128, 8 * D], BF16))
            wk_sb = pa.enter_context(nc.sbuf_tensor("wk_sb", [128, 8 * D], BF16))
            wv_sb = pa.enter_context(nc.sbuf_tensor("wv_sb", [128, 8 * D], BF16))
            wo_sb = pa.enter_context(nc.sbuf_tensor("wo_sb", [128, 8 * D], BF16))
            gw_sb = pa.enter_context(nc.sbuf_tensor("gw_sb", [128, 8 * NE], BF16))
            aT_sb = pa.enter_context(nc.sbuf_tensor("aT_sb", [128, 8 * 128], BF16))

            x_blk = [pa.enter_context(nc.sbuf_tensor(f"x_blk{i}", [128, 4 * D], F32)) for i in range(2)]
            xn_o = pa.enter_context(nc.sbuf_tensor("xn_o", [128, 4 * D], BF16))
            o_sb = pa.enter_context(nc.sbuf_tensor("o_sb", [128, 4 * D], BF16))
            lnst = pa.enter_context(nc.sbuf_tensor("lnst", [128, 32], F32))
            xnT_blk = pa.enter_context(nc.sbuf_tensor("xnT_blk", [128, 8 * W], BF16))
            qoT_blk = pa.enter_context(nc.sbuf_tensor("qoT_blk", [128, 8 * W], BF16))
            kT_blk = [pa.enter_context(nc.sbuf_tensor(f"kT_blk{i}", [128, 8 * W], BF16)) for i in range(2)]
            vaug = [pa.enter_context(nc.sbuf_tensor(f"vaug{i}", [128, 4 * H * 65], BF16)) for i in range(2)]
            expT = [pa.enter_context(nc.sbuf_tensor(f"expT{i}", [128, 8 * W], BF16)) for i in range(3)]
            attnp_ev = [pa.enter_context(nc.sbuf_tensor(f"attnp_ev{i}", [128, D], F32)) for i in range(2)]
            gt = pa.enter_context(nc.sbuf_tensor("gt", [128, 2 * NE], F32))
            gs = pa.enter_context(nc.sbuf_tensor("gs", [128, 8], F32))
            m1m2 = pa.enter_context(nc.sbuf_tensor("m1m2", [128, 4 * 2 * NE], BF16))
            m1T_sb = pa.enter_context(nc.sbuf_tensor("m1T_sb", [NE, W], BF16))
            m2T_sb = pa.enter_context(nc.sbuf_tensor("m2T_sb", [NE, W], BF16))
            s_sb = pa.enter_context(nc.sbuf_tensor("s_sb", [128, W], BF16))
            s12_sb = pa.enter_context(nc.sbuf_tensor("s12_sb", [128, W], BF16))
            rcp = pa.enter_context(nc.sbuf_tensor("rcp", [128, 8], F32))

            ps_ch = [pa.enter_context(nc.psum_tensor(f"ps_ch{i}", [128, 512], F32)) for i in range(2)]
            ps_sc = [pa.enter_context(nc.psum_tensor(f"ps_sc{i}", [128, 512], F32)) for i in range(2)]
            ps_pv = [pa.enter_context(nc.psum_tensor(f"ps_pv{i}", [128, 512], F32)) for i in range(2)]
            ps_tr = [pa.enter_context(nc.psum_tensor(f"ps_tr{i}", [128, 1024], BF16)) for i in range(2)]

            B_x = [Buf(), Buf()]
            B_xn = Buf()
            B_o = Buf()
            B_xnT = Buf()
            B_qoT = Buf()
            B_kT = [Buf(), Buf()]
            B_vaug = [Buf(), Buf()]
            B_expT = [Buf(), Buf(), Buf()]
            B_attnp = [Buf(), Buf()]
            B_pch = [Buf(), Buf()]
            B_psc = [Buf(), Buf()]
            B_ppv = [Buf(), Buf()]
            B_ptr = [Buf(), Buf()]
            B_m1m2 = Buf()
            B_m1T = Buf()
            B_m2T = Buf()
            B_s = Buf()
            B_s12 = Buf()
            B_gs = Buf()

            for par in range(2):
                def setones(e, pv=par):
                    v = vaug[pv][:, :].rearrange("p (c x) -> p c x", x=65)
                    return e.memset(v[:, :, 64:65], 1.0)
                B_vaug[par].set_ready(em.emit("vector", setones, inc=sVE))

            wtok = {}
            for nm, dst, src in (("wq", wq_sb, wqT), ("wk", wk_sb, wkT),
                                 ("wv", wv_sb, wvT), ("wo", wo_sb, woT)):
                t = None
                for k4 in range(2):
                    t = em.emit("gpsimd", lambda e, d=dst, s=src, k=k4: e.dma_start(
                        out=d[:, k * 4 * D:(k + 1) * 4 * D].rearrange("p (c d) -> p c d", d=D),
                        in_=s[k * 512:(k + 1) * 512, :].rearrange("(c p) d -> p c d", p=128)), inc=sQG)
                wtok[nm] = t
            for kd in range(8):
                em.emit("gpsimd", lambda e, k=kd: e.dma_start(
                    out=gw_sb[:, k * NE:(k + 1) * NE], in_=gwT_in[k * 128:(k + 1) * 128, :]), inc=sQW)
            t_gw = (sQW, sQW.n)
            t_gb = consts_tok2
            for kd in range(8):
                t_aT = em.emit("gpsimd", lambda e, k=kd: e.dma_start(
                    out=aT_sb[:, k * 128:(k + 1) * 128], in_=aT_in[k * 128:(k + 1) * 128, :]), inc=sQG)
            wq_tok, wk_tok, wv_tok, wo_tok = wtok["wq"], wtok["wk"], wtok["wv"], wtok["wo"]

            state = {"t_xn": None, "tel": None, "m12": [None] * 4, "sblk": None}

            def emit_prologue_a1(s, n):
                bi = s * NBLK + n
                par = bi % 2
                st_t_xn = state["t_xn"]
                st_prev_tel = state["tel"]
                t_x = em.emit("sync", lambda e, ss=s, nn=n, p=par: e.dma_start(
                    out=x_blk[p][:, :].rearrange("p (c d) -> p c d", d=D),
                    in_=x_in[ss * T + nn * W: ss * T + (nn + 1) * W, :]
                        .rearrange("(c p) d -> p c d", p=128)),
                    waits=dedup(B_x[par].wr_waits()), inc=sQL)
                B_x[par].set_ready(t_x)

                xn_w = dedup(B_xn.wr_waits())
                for c in range(4):
                    L = 8 * c
                    xc = x_blk[par][:, c * D:(c + 1) * D]
                    em.emit("vector", lambda e, xc=xc, L=L: e.reduce_sum(
                        lnst[:, L:L + 1], xc, axis=AX.X),
                        waits=dedup([t_x, st_t_xn]))
                    t_sq = em.emit("scalar", lambda e, xc=xc, cc=c, L=L: e.activation(
                        xn_o[:, cc * D:(cc + 1) * D], xc, AF.Square, accum_out=lnst[:, L + 1:L + 2]),
                        waits=dedup([t_x, st_t_xn] + (xn_w if c == 0 else [])), inc=sSE)
                    em.emit("vector", lambda e, L=L: e.tensor_scalar_mul(
                        lnst[:, L + 2:L + 3], lnst[:, L:L + 1], 1.0 / D), dr=True)
                    em.emit("vector", lambda e, L=L: e.tensor_scalar(
                        lnst[:, L + 3:L + 4], lnst[:, L:L + 1], lnst[:, L:L + 1], 1.0 / (D * D),
                        ALU.mult, ALU.mult))
                    t_v = em.emit("vector", lambda e, L=L: e.tensor_scalar(
                        lnst[:, L + 4:L + 5], lnst[:, L + 1:L + 2], 1.0 / D, lnst[:, L + 3:L + 4],
                        ALU.mult, ALU.subtract),
                        waits=[t_sq], inc=sVE, dr=True)
                    em.emit("scalar", lambda e, L=L: e.activation(
                        lnst[:, L + 5:L + 6], lnst[:, L + 4:L + 5], AF.Ln, bias=fc_sb[:, 1:2]),
                        waits=dedup([t_v, t_fc]))
                    t_rs = em.emit("scalar", lambda e, L=L: e.activation(
                        lnst[:, L + 6:L + 7], lnst[:, L + 5:L + 6], AF.Exp, scale=-0.5), inc=sSE, dr=True)
                    t_nm = em.emit("vector", lambda e, L=L: e.tensor_scalar(
                        lnst[:, L + 7:L + 8], lnst[:, L + 2:L + 3], lnst[:, L + 6:L + 7], -1.0,
                        ALU.mult, ALU.mult),
                        waits=[t_rs], inc=sVE)
                    st_t_xn = em.emit("scalar", lambda e, xc=xc, cc=c, L=L: e.activation(
                        xn_o[:, cc * D:(cc + 1) * D], xc, AF.Identity,
                        bias=lnst[:, L + 7:L + 8], scale=lnst[:, L + 6:L + 7]),
                        waits=dedup([t_nm] + xn_w), inc=sSE, dr=True)
                B_xn.set_ready(st_t_xn)
                B_x[par].add_reader(st_t_xn)
                if dbg and bi == 0:
                    t_d1 = em.emit("scalar", lambda e: e.dma_start(out=dbgx[:, :], in_=x_blk[0][:, :]),
                                   waits=[st_t_xn], inc=sQS)
                    t_d2 = em.emit("scalar", lambda e: e.dma_start(out=dbgxn[:, :], in_=xn_o[:, :]),
                                   waits=[st_t_xn], inc=sQS)
                    B_x[0].add_reader(t_d1)
                    B_xn.add_reader(t_d2)

                state["t_xn"] = st_t_xn
                state["tel"] = st_prev_tel

            def emit_prologue_a2(s, n):
                bi = s * NBLK + n
                par = bi % 2
                st_t_xn = state["t_xn"]
                st_prev_tel = state["tel"]
                xnT_w = dedup(B_xnT.wr_waits())
                t_ev = None
                for c in range(4):
                    for kd in range(8):
                        i = (c * 8 + kd) % 2
                        tw = dedup([B_xn.ready[0], consts_tok] + B_ptr[i].wr_waits() +
                                   (xnT_w if (c, kd) == (0, 0) else []))
                        t_tr = em.emit("tensor", lambda e, cc=c, k=kd, ii=i: e.transpose(
                            ps_tr[ii][:, 0:128],
                            xn_o[:, cc * D + k * 128: cc * D + k * 128 + 128], ident[:, :]),
                            waits=tw, inc=sPE)
                        B_ptr[i].set_ready(t_tr)
                        t_ev = em.emit("vector", lambda e, cc=c, k=kd, ii=i: e.tensor_copy(
                            xnT_blk[:, k * W + cc * 128: k * W + cc * 128 + 128], ps_tr[ii][:, 0:128]),
                            waits=[t_tr], inc=sVE, war=((c, kd) != (3, 7)))
                        B_ptr[i].add_reader(t_ev)
                B_xnT.set_ready(t_ev)
                B_xn.add_reader(t_ev)

                t_st = em.emit("scalar", lambda e, ss=s, nn=n: e.dma_start(
                    out=xnT_scr[ss, :, :, nn * W:(nn + 1) * W].rearrange("k p t -> p k t"),
                    in_=xnT_blk[:, :].rearrange("p (k t) -> p k t", t=W)),
                    waits=[B_xnT.ready[0]], inc=sQS)
                B_xnT.add_reader(t_st)

                for c in range(4):
                    pc = 1
                    gw_w = dedup(B_pch[pc].wr_waits() + [B_xnT.ready[0], t_gw, t_gb, consts_tok])
                    for kd in range(8):
                        em.emit("tensor", lambda e, k=kd, cc=c, p=pc: e.matmul(
                            ps_ch[p][:, 0:NE],
                            xnT_blk[:, k * W + cc * 128: k * W + cc * 128 + 128],
                            gw_sb[:, k * NE:(k + 1) * NE],
                            start=(k == 0), stop=False), waits=gw_w if kd == 0 else ())
                    t_g = em.emit("tensor", lambda e, p=pc: e.matmul(
                        ps_ch[p][:, 0:NE], ones_row[0:1, 0:128], gb_sb[0:1, :],
                        start=False, stop=True), inc=sPE)
                    B_pch[pc].set_ready(t_g)
                    gsw = dedup([t_g, st_prev_tel] + B_gs.wr_waits())
                    em.emit("vector", lambda e, p=pc: e.reduce_max(
                        gs[:, 0:1], ps_ch[p][:, 0:NE], axis=AX.X), waits=gsw)
                    em.emit("vector", lambda e: e.tensor_scalar_mul(gs[:, 1:2], gs[:, 0:1], -1.0), dr=True)
                    t_m1 = em.emit("vector", lambda e, p=pc, cc=c: e.tensor_scalar(
                        m1m2[:, cc * 2 * NE: cc * 2 * NE + NE], ps_ch[p][:, 0:NE], gs[:, 0:1], None, ALU.is_ge),
                        waits=dedup(B_m1m2.wr_waits() if c == 0 else []), inc=sVE)
                    t_el = em.emit("scalar", lambda e, p=pc: e.activation(
                        gt[:, 0:NE], ps_ch[p][:, 0:NE], AF.Exp, bias=gs[:, 1:2]),
                        waits=[t_m1], inc=sSE)
                    st_prev_tel = t_el
                    B_pch[pc].add_reader(t_el)
                    em.emit("vector", lambda e: e.reduce_sum(gs[:, 2:3], gt[:, 0:NE], axis=AX.X),
                            waits=[t_el])
                    em.emit("vector", lambda e, cc=c: e.scalar_tensor_tensor(
                        gt[:, NE:2 * NE], m1m2[:, cc * 2 * NE: cc * 2 * NE + NE], -2.0, gt[:, 0:NE], ALU.mult, ALU.add))
                    em.emit("vector", lambda e: e.reduce_max(gs[:, 4:5], gt[:, NE:2 * NE], axis=AX.X), dr=True)
                    t_m2 = em.emit("vector", lambda e, cc=c: e.tensor_scalar(
                        m1m2[:, cc * 2 * NE + NE: (cc + 1) * 2 * NE], gt[:, NE:2 * NE], gs[:, 4:5], None, ALU.is_ge), inc=sVE, dr=True)
                    em.emit("vector", lambda e: e.reciprocal(gs[:, 3:4], gs[:, 2:3]))
                    t_vs = em.emit("vector", lambda e, col=bi * 4 + c: e.tensor_scalar(
                        vs_all[:, col:col + 1], gs[:, 4:5], 1.0, gs[:, 3:4], ALU.add, ALU.mult), inc=sVE, dr=True)
                    B_gs.set_ready(t_vs)
                    state["m12"][c] = (t_m1, t_m2)
                    B_m1m2.set_ready(t_m1)

                state["t_xn"] = st_t_xn
                state["tel"] = st_prev_tel


            def emit_prologue_b(s, n):
                bi = s * NBLK + n
                for c in range(4):
                    t_m1, t_m2 = state["m12"][c]
                    for j, (off, mdstT, mbuf) in enumerate(((0, m1T_sb, B_m1T), (NE, m2T_sb, B_m2T))):
                        pt = (c * 2 + j) % 2
                        tw = dedup([t_m1, t_m2, consts_tok] + B_ptr[pt].wr_waits() +
                                   (mbuf.wr_waits() if c == 0 else []))
                        t_tr = em.emit("tensor", lambda e, o=off, p=pt, cc=c: e.transpose(
                            ps_tr[p][0:NE, 0:128], m1m2[:, cc * 2 * NE + o: cc * 2 * NE + o + NE], ident[:, :]),
                            waits=tw, inc=sPE)
                        B_ptr[pt].set_ready(t_tr)
                        t_mev = em.emit("vector", lambda e, dT=mdstT, cc=c, p=pt: e.tensor_copy(
                            dT[:, cc * 128:(cc + 1) * 128], ps_tr[p][0:NE, 0:128]),
                            waits=[t_tr], inc=sVE)
                        B_ptr[pt].add_reader(t_mev)
                        if c == 3:
                            mbuf.set_ready(t_mev)
                        if j == 1:
                            B_m1m2.add_reader(t_tr)
                sw = dedup(B_pch[0].wr_waits() + [B_xnT.ready[0], t_aT])
                t_schain = None
                for kd in range(8):
                    t_schain = em.emit("tensor", lambda e, k=kd: e.matmul(
                        ps_ch[0][:, 0:W], aT_sb[:, k * 128:(k + 1) * 128], xnT_blk[:, k * W:(k + 1) * W],
                        start=(k == 0), stop=(k == 7)),
                        waits=sw if kd == 0 else (), inc=sPE if kd == 7 else None)
                B_pch[0].set_ready(t_schain)
                t_sev = em.emit("vector", lambda e: e.tensor_copy(s_sb[:, :], ps_ch[0][:, 0:W]),
                                waits=dedup([t_schain] + B_s.wr_waits()), inc=sVE)
                B_s.set_ready(t_sev)
                B_pch[0].add_reader(t_sev)
                for j, (mT, mbuf, scr) in enumerate(
                        ((m1T_sb, B_m1T, s1_scr), (m2T_sb, B_m2T, s2_scr))):
                    dst = s12_sb if j == 0 else s_sb
                    dbuf2 = B_s12 if j == 0 else B_s
                    rw = dedup(B_pch[1].wr_waits() + mbuf.ready + [consts_tok])
                    t_rep = em.emit("tensor", lambda e, mm=mT: e.matmul(
                        ps_ch[1][:, 0:W], rep_sb[:, :], mm[:, :], start=True, stop=True),
                        waits=rw, inc=sPE)
                    B_pch[1].set_ready(t_rep)
                    mbuf.add_reader(t_rep)
                    t_s12 = em.emit("vector", lambda e, d=dst: e.tensor_tensor(
                        d[:, 0:W], s_sb[:, :], ps_ch[1][:, 0:W], ALU.mult),
                        waits=dedup([t_rep, t_sev] + (B_s12.wr_waits() if j == 0 else [])), inc=sVE)
                    B_pch[1].add_reader(t_s12)
                    B_s.add_reader(t_s12)
                    dbuf2.set_ready(t_s12)
                    t_st = em.emit("scalar", lambda e, d=dst, sc=scr, ss=s, nn=n: e.dma_start(
                        out=sc[:, ss * T + nn * W: ss * T + (nn + 1) * W],
                        in_=d[:, 0:W]), waits=[t_s12], inc=sQS)
                    dbuf2.add_reader(t_st)




            kT_toks = [[None] * 8, [None] * 8]

            emit_prologue_a1(0, 0)
            emit_prologue_a2(0, 0)
            emit_prologue_b(0, 0)
            for s in range(SEQ):
                for n in range(NBLK):
                    bi = s * NBLK + n
                    par = bi % 2
                    bpar = n % 2

                    qT_toks = [None] * 8
                    kT_toks_cur = [None] * 8
                    for m in range(8):
                        for which, (wsb, wtk, scale) in enumerate(
                                ((wq_sb, wq_tok, 0.125), (wk_sb, wk_tok, None))):
                            dstT = qoT_blk if which == 0 else kT_blk[bpar]
                            dbuf = B_qoT if which == 0 else B_kT[bpar]
                            if m == 0:
                                if which == 0:
                                    qoT_w0 = dedup(dbuf.wr_waits())
                                else:
                                    kT_w0 = dedup(dbuf.wr_waits())
                            dw = qoT_w0 if which == 0 else kT_w0
                            pc = (m * 2 + which) % 2
                            mw = dedup(B_pch[pc].wr_waits() + [B_xnT.ready[0], wtk])
                            t_mm = None
                            for kd in range(8):
                                t_mm = em.emit("tensor", lambda e, k=kd, mm=m, p=pc, ws=wsb: e.matmul(
                                    ps_ch[p][:, 0:W], ws[:, k * D + mm * 128: k * D + mm * 128 + 128],
                                    xnT_blk[:, k * W:(k + 1) * W],
                                    start=(k == 0), stop=(k == 7)),
                                    waits=mw if kd == 0 else (), inc=sPE if kd == 7 else None)
                            B_pch[pc].set_ready(t_mm)
                            if scale is not None:
                                t_ev = em.emit("vector", lambda e, mm=m, p=pc, dT=dstT: e.tensor_scalar_mul(
                                    dT[:, mm * W:(mm + 1) * W], ps_ch[p][:, 0:W], 0.125),
                                    waits=dedup([t_mm] + (dw if m == 0 else [])), inc=sVE)
                                qT_toks[m] = t_ev
                            else:
                                t_ev = em.emit("vector", lambda e, mm=m, p=pc, dT=dstT: e.tensor_copy(
                                    dT[:, mm * W:(mm + 1) * W], ps_ch[p][:, 0:W]),
                                    waits=dedup([t_mm] + (dw if m == 0 else [])), inc=sVE)
                                kT_toks_cur[m] = t_ev
                            B_pch[pc].add_reader(t_ev)
                    B_qoT.set_ready(qT_toks[7])
                    B_kT[bpar].set_ready(kT_toks_cur[7])
                    kT_toks[bpar] = kT_toks_cur
                    vg_w = dedup(B_vaug[bpar].wr_waits())
                    t_vev = None
                    for c in range(4):
                        for hf in range(2):
                            pc = (c * 2 + hf) % 2
                            mw = dedup(B_pch[pc].wr_waits() + [B_xnT.ready[0], wv_tok])
                            t_mm = None
                            for kd in range(8):
                                t_mm = em.emit("tensor", lambda e, k=kd, cc=c, h=hf, p=pc: e.matmul(
                                    ps_ch[p][:, 0:W],
                                    xnT_blk[:, k * W + cc * 128: k * W + cc * 128 + 128],
                                    wv_sb[:, k * D + h * W: k * D + (h + 1) * W],
                                    start=(k == 0), stop=(k == 7)),
                                    waits=mw if kd == 0 else (), inc=sPE if kd == 7 else None)
                            B_pch[pc].set_ready(t_mm)

                            def vev(e, cc=c, h=hf, p=pc, bp=bpar):
                                base = cc * H * 65 + h * 8 * 65
                                dst = vaug[bp][:, base: base + 8 * 65]
                                dstv = dst.rearrange("p (hh x) -> p hh x", x=65)[:, :, 0:64]
                                srcv = ps_ch[p][:, 0:W].rearrange("p (hh x) -> p hh x", x=64)
                                return e.tensor_copy(dstv, srcv)
                            t_vev = em.emit("vector", vev,
                                            waits=dedup([t_mm] + (vg_w if (c, hf) == (0, 0) else [])),
                                            inc=sVE, war=((c, hf) != (3, 1)))
                            B_pch[pc].add_reader(t_vev)
                    B_vaug[bpar].set_ready(t_vev)
                    B_xnT.add_reader(t_vev)
                    if bi > 0:
                        emit_prologue_b(s, n)

                    sc_pool = [ps_sc[0], ps_sc[1], ps_ch[0], ps_ch[1]]
                    sc_bufs = [B_psc[0], B_psc[1], B_pch[0], B_pch[1]]
                    o_w = dedup(B_o.wr_waits())
                    rcp_w = [[] for _ in range(8)]
                    msk_tok = [None, None, None]
                    last_o = [None]

                    def emit_scores(h, pv_h=None, n=n, bpar=bpar):
                        hp = h % 3
                        rr = sc_ranges(n)
                        pace = max(1, len(rr) // 4)
                        hb = (h % 2) * 64
                        ck = h // 2
                        ew = dedup(B_expT[hp].wr_waits())
                        t_exp = None
                        t_msk = None
                        for ri, (kind, jc, i0, wdt) in enumerate(rr):
                            slot = jc if kind == "prev" else 4 + jc
                            kbuf = kT_blk[(n - 1) % 2] if kind == "prev" else kT_blk[bpar]
                            ck2 = h // 2
                            if kind == "prev":
                                ktoks = [kT_toks[(n - 1) % 2][ck2]]
                            else:
                                ktoks = [kT_toks[bpar][ck2]]
                            psc = ri % 4
                            pbank = sc_pool[psc]
                            pbuf = sc_bufs[psc]
                            mw = dedup(pbuf.wr_waits() + ktoks + [qT_toks[ck2]])
                            t_mm = em.emit("tensor", lambda e, kb=kbuf, b=hb, c=ck, j=jc, ii=i0, wd=wdt, pb=pbank: e.matmul(
                                pb[:, ii:ii + wd],
                                kb[b:b + 64, c * W + j * 128: c * W + j * 128 + 128],
                                qoT_blk[b:b + 64, c * W + ii: c * W + ii + wd],
                                start=True, stop=True),
                                waits=mw, inc=sPE)
                            pbuf.set_ready(t_mm)
                            first = (kind, jc) == (("own", 0) if n == 0 else ("prev", 0))
                            t_exp = em.emit("scalar", lambda e, sl=slot, ii=i0, wd=wdt, pb=pbank, hh=hp: e.activation(
                                expT[hh][:, sl * W + ii: sl * W + ii + wd],
                                pb[:, ii:ii + wd], AF.Exp, bias=fc_sb[:, 0:1]),
                                waits=dedup([t_mm, t_fc] + (ew if first else [])),
                                inc=sSE, war=(ri != len(rr) - 1))
                            pbuf.add_reader(t_exp)
                            if pv_h is not None and ri % pace == pace - 1 and ri // pace < 4:
                                emit_pv_ic(pv_h, ri // pace)
                        for mi, (kind, jc, i0, wdt) in enumerate(sc_ranges(n)):
                            slot = jc if kind == "prev" else 4 + jc
                            mk = maskA if kind == "prev" else maskB
                            dcol = slot * W + jc * 128
                            t_msk = em.emit("gpsimd", lambda e, dc=dcol, m=mk, hh=hp: e.tensor_mul(
                                expT[hh][:, dc:dc + 128], expT[hh][:, dc:dc + 128], m[:, :]),
                                waits=dedup([t_exp, consts_tok]),
                                inc=sGP if mi == len(sc_ranges(n)) - 1 else None)
                        B_expT[hp].set_ready(t_msk)
                        return t_msk

                    def emit_pv_ic(h, ic, n=n, bpar=bpar, last=False):
                        hp = h % 3
                        t_msk = msk_tok[hp]
                        if True:
                            tiles = ([("prev", jc) for jc in range(ic, 4)] if n > 0 else []) + \
                                    [("own", jc) for jc in range(0, ic + 1)]
                            pp = ic % 2
                            mw = dedup(B_ppv[pp].wr_waits() + [t_msk] +
                                       B_vaug[bpar].ready + (B_vaug[(n - 1) % 2].ready if n > 0 else []))
                            t_pv = None
                            for ti, (kind, jc) in enumerate(tiles):
                                slot = jc if kind == "prev" else 4 + jc
                                vb = vaug[(n - 1) % 2] if kind == "prev" else vaug[bpar]
                                t_pv = em.emit("tensor", lambda e, sl=slot, j=jc, icc=ic, v=vb, hh=hp, p=pp, hd=h,
                                               st=(ti == 0), sp=(ti == len(tiles) - 1): e.matmul(
                                    ps_pv[p][:, 0:65],
                                    expT[hh][:, sl * W + icc * 128: sl * W + icc * 128 + 128],
                                    v[:, j * H * 65 + hd * 65: j * H * 65 + hd * 65 + 65],
                                    start=st, stop=sp),
                                    waits=mw if ti == 0 else (),
                                    inc=sPE if ti == len(tiles) - 1 else None)
                            B_ppv[pp].set_ready(t_pv)
                            B_vaug[bpar].add_reader(t_pv)
                            if n > 0:
                                B_vaug[(n - 1) % 2].add_reader(t_pv)
                            rc = (h * 4 + ic) % 8
                            em.emit("vector", lambda e, p=pp, r=rc: e.reciprocal(
                                rcp[:, r:r + 1], ps_pv[p][:, 64:65]),
                                waits=dedup([t_pv] + (o_w if (h, ic) == (0, 0) else [])))
                            t_o = em.emit("vector", lambda e, icc=ic, hd=h, p=pp, r=rc: e.tensor_scalar(
                                o_sb[:, icc * D + hd * 64: icc * D + hd * 64 + 64],
                                ps_pv[p][:, 0:64], rcp[:, r:r + 1], None, ALU.mult),
                                inc=sVE, dr=True, war=(not (last and ic == 3)))
                            B_ppv[pp].add_reader(t_o)
                            B_expT[hp].add_reader(t_o)
                            last_o[0] = t_o

                    def emit_pv(h, last=False):
                        for ic in range(4):
                            emit_pv_ic(h, ic, last=last)

                    for h in range(H):
                        msk_tok[h % 3] = emit_scores(h, pv_h=(h - 2 if h >= 2 else None))
                        if h == 2 and bi + 1 < SEQ * NBLK:
                            emit_prologue_a1((bi + 1) // NBLK, (bi + 1) % NBLK)
                        if h == 9 and bi + 1 < SEQ * NBLK:
                            emit_prologue_a2((bi + 1) // NBLK, (bi + 1) % NBLK)
                    emit_pv(H - 2)
                    emit_pv(H - 1, last=True)
                    t_o = last_o[0]
                    B_o.set_ready(t_o)
                    t_attn = (sPE, sPE.n)
                    B_qoT.add_reader(t_attn)
                    B_kT[0].add_reader(t_attn)
                    B_kT[1].add_reader(t_attn)

                    qw = dedup(B_qoT.wr_waits())
                    t_ev = None
                    for c in range(4):
                        for kd in range(8):
                            i = (c * 8 + kd) % 2
                            tw = dedup([B_o.ready[0], consts_tok] + B_ptr[i].wr_waits() +
                                       (qw if (c, kd) == (0, 0) else []))
                            t_tr = em.emit("tensor", lambda e, cc=c, k=kd, ii=i: e.transpose(
                                ps_tr[ii][:, 0:128],
                                o_sb[:, cc * D + k * 128: cc * D + k * 128 + 128], ident[:, :]),
                                waits=tw, inc=sPE)
                            B_ptr[i].set_ready(t_tr)
                            t_ev = em.emit("vector", lambda e, cc=c, k=kd, ii=i: e.tensor_copy(
                                qoT_blk[:, k * W + cc * 128: k * W + cc * 128 + 128], ps_tr[ii][:, 0:128]),
                                waits=[t_tr], inc=sVE, war=((c, kd) != (3, 7)))
                            B_ptr[i].add_reader(t_ev)
                    B_qoT.set_ready(t_ev)
                    B_o.add_reader(t_ev)

                    for c in range(4):
                        ap_par = c % 2
                        apw = dedup(B_attnp[ap_par].wr_waits())
                        t_ap = None
                        for hf in range(2):
                            pc = (c * 2 + hf) % 2
                            mw = dedup(B_pch[pc].wr_waits() + B_qoT.ready + [wo_tok])
                            t_mm = None
                            for kd in range(8):
                                t_mm = em.emit("tensor", lambda e, k=kd, cc=c, h=hf, p=pc: e.matmul(
                                    ps_ch[p][:, 0:W],
                                    qoT_blk[:, k * W + cc * 128: k * W + cc * 128 + 128],
                                    wo_sb[:, k * D + h * W: k * D + (h + 1) * W],
                                    start=(k == 0), stop=(k == 7)),
                                    waits=mw if kd == 0 else (), inc=sPE if kd == 7 else None)
                            B_pch[pc].set_ready(t_mm)
                            B_qoT.add_reader(t_mm)
                            t_ap = em.emit("vector", lambda e, cc=c, h=hf, p=pc, ap=ap_par, px=par: e.tensor_tensor(
                                attnp_ev[ap][:, h * W:(h + 1) * W], ps_ch[p][:, 0:W],
                                x_blk[px][:, cc * D + h * W: cc * D + h * W + W], ALU.add),
                                waits=dedup([t_mm] + B_x[par].ready + (apw if hf == 0 else [])), inc=sVE)
                            B_pch[pc].add_reader(t_ap)
                            B_x[par].add_reader(t_ap)
                        B_attnp[ap_par].set_ready(t_ap)
                        t_st = em.emit("scalar", lambda e, cc=c, ss=s, nn=n, ap=ap_par: e.dma_start(
                            out=attnp_scr[ss * T + nn * W + cc * 128: ss * T + nn * W + (cc + 1) * 128, :],
                            in_=attnp_ev[ap][:, :]), waits=[t_ap], inc=sQS)
                        B_attnp[ap_par].add_reader(t_st)

            feA = [(sPE, sPE.n), (sVE, sVE.n), (sSE, sSE.n), (sQS, sQS.n), (sQL, sQL.n)]

        pb = contextlib.ExitStack()
        with pb:
            wi_sb = pb.enter_context(nc.sbuf_tensor("wi_sb", [128, 8 * DFF], BF16))
            wio_sb = pb.enter_context(nc.sbuf_tensor("wio_sb", [128, NFC * D], BF16))
            bst_sb = pb.enter_context(nc.sbuf_tensor("bst_sb", [128, DFF], BF16))
            xnT_f = [pb.enter_context(nc.sbuf_tensor(f"xnT_f{i}", [128, 8 * TT], BF16)) for i in range(2)]
            s1f = [pb.enter_context(nc.sbuf_tensor(f"s1f{i}", [128, TT], BF16)) for i in range(2)]
            s2f = [pb.enter_context(nc.sbuf_tensor(f"s2f{i}", [128, TT], BF16)) for i in range(2)]
            hsum = pb.enter_context(nc.sbuf_tensor("hsum", [128, NFC * TT], BF16))
            attnp_f = [pb.enter_context(nc.sbuf_tensor(f"attnp_f{i}", [128, 2 * D], F32)) for i in range(2)]

            ps_z = [pb.enter_context(nc.psum_tensor(f"ps_z{i}", [128, 512], F32)) for i in range(4)]
            ps_wo = [pb.enter_context(nc.psum_tensor(f"ps_wo{i}", [128, 512], F32)) for i in range(4)]

            t_wi = t_wo2 = None
            for k4 in range(2):
                t_wi = em.emit("gpsimd", lambda e, k=k4: e.dma_start(
                    out=wi_sb[:, k * 4 * DFF:(k + 1) * 4 * DFF].rearrange("p (c d) -> p c d", d=DFF),
                    in_=wiT_in[k * 512:(k + 1) * 512, :].rearrange("(c p) d -> p c d", p=128)),
                    waits=feA if k4 == 0 else (), inc=sQG)
            for fk in range(8):
                t_wo2 = em.emit("gpsimd", lambda e, k=fk: e.dma_start(
                    out=wio_sb[:, k * 4 * D:(k + 1) * 4 * D].rearrange("p (c d) -> p c d", d=D),
                    in_=wioT_in[k * 512:(k + 1) * 512, :].rearrange("(c p) d -> p c d", p=128)), inc=sQG)
            t_bst = em.emit("gpsimd", lambda e: e.dma_start(out=bst_sb[:, :], in_=bst_in[:, :]), inc=sQG)
            t_bi = consts_tok
            h2_sb = [pb.enter_context(nc.sbuf_tensor(f"h2_sb{i}", [128, TT], BF16)) for i in range(2)]
            base_bf = [pb.enter_context(nc.sbuf_tensor(f"base_bf{i}", [128, TT], BF16)) for i in range(2)]
            B_h2 = [Buf(), Buf()]
            B_base = [Buf(), Buf()]

            B_xf = [Buf(), Buf()]
            B_s1 = [Buf(), Buf()]
            B_s2 = [Buf(), Buf()]
            B_hs = Buf()
            B_af = [Buf(), Buf()]
            B_z = [Buf() for _ in range(4)]
            B_w = [Buf() for _ in range(4)]

            for tl in range(NTILE):
                s = tl // (T // TT)
                t0 = (tl % (T // TT)) * TT
                par = tl % 2

                t_lx = em.emit("sync", lambda e, ss=s, tt=t0, p=par: e.dma_start(
                    out=xnT_f[p][:, :].rearrange("p (k t) -> p k t", t=TT),
                    in_=xnT_scr[ss, :, :, tt:tt + TT].rearrange("k p t -> p k t")),
                    waits=dedup(B_xf[par].wr_waits() + (feA if tl == 0 else [])), inc=sQL)
                B_xf[par].set_ready(t_lx)
                t_l1 = em.emit("sync", lambda e, ss=s, tt=t0, p=par: e.dma_start(
                    out=s1f[p][:, :], in_=s1_scr[:, ss * T + tt: ss * T + tt + TT]),
                    waits=dedup(B_s1[par].wr_waits() + (feA if tl == 0 else [])), inc=sQL)
                B_s1[par].set_ready(t_l1)
                t_l2 = em.emit("sync", lambda e, ss=s, tt=t0, p=par: e.dma_start(
                    out=s2f[p][:, :], in_=s2_scr[:, ss * T + tt: ss * T + tt + TT]),
                    waits=dedup(B_s2[par].wr_waits() + (feA if tl == 0 else [])), inc=sQL)
                B_s2[par].set_ready(t_l2)
                t_la = em.emit("sync", lambda e, ss=s, tt=t0, p=par: e.dma_start(
                    out=attnp_f[p][:, :].rearrange("p (c d) -> p c d", d=D),
                    in_=attnp_scr[ss * T + tt: ss * T + tt + TT, :].rearrange("(c p) d -> p c d", p=128)),
                    waits=dedup(B_af[par].wr_waits() + (feA if tl == 0 else [])), inc=sQL)
                B_af[par].set_ready(t_la)

                hs_w = dedup(B_hs.wr_waits())
                t_h2 = None

                def emit_base(fc):
                    fp = fc % 2
                    bb = fp * 3
                    mw = dedup(B_z[bb].wr_waits() + [t_lx, t_wi])
                    for kd in range(8):
                        em.emit("tensor", lambda e, k=kd, f=fc, z=bb, p=par: e.matmul(
                            ps_z[z][:, 0:TT],
                            wi_sb[:, k * DFF + f * 128: k * DFF + f * 128 + 128],
                            xnT_f[p][:, k * TT:(k + 1) * TT],
                            start=(k == 0), stop=(k == 7)),
                            waits=mw if kd == 0 else (), inc=sPE if kd == 7 else None)
                    t_bmm = (sPE, sPE.n)
                    B_z[bb].set_ready(t_bmm)
                    t_bev = em.emit("vector", lambda e, z=bb, p=fp: e.tensor_copy(
                        base_bf[p][:, :], ps_z[z][:, 0:TT]),
                        waits=dedup([t_bmm] + B_base[fp].wr_waits()), inc=sVE)
                    B_z[bb].add_reader(t_bev)
                    B_base[fp].set_ready(t_bev)

                def emit_z(fc):
                    fp = fc % 2
                    for which, (sfb, stok, lb) in enumerate(((s1f[par], t_l1, 1), (s2f[par], t_l2, 2))):
                        em.emit("tensor", lambda e, p=fp, z=lb: e.matmul(
                            ps_z[z][:, 0:TT], ident[:, :], base_bf[p][:, :], start=True, stop=False),
                            waits=dedup(B_z[lb].wr_waits() + B_base[fp].ready + [consts_tok]))
                        t_z = em.emit("tensor", lambda e, f=fc, z=lb, sf=sfb: e.matmul(
                            ps_z[z][:, 0:TT], bst_sb[:, f * 128:(f + 1) * 128], sf[:, :],
                            start=False, stop=True),
                            waits=dedup([stok, t_bst]), inc=sPE)
                        B_z[lb].set_ready(t_z)
                        if which == 1:
                            B_base[fp].add_reader(t_z)
                    t_h1 = em.emit("scalar", lambda e, f=fc: e.activation(
                        hsum[:, f * TT:(f + 1) * TT], ps_z[1][:, 0:TT], AF.Relu,
                        bias=bi2_sb[:, f:f + 1]),
                        waits=dedup([B_z[1].ready[0], t_bi] + (hs_w if fc == 0 else [])))
                    t_hb = em.emit("scalar", lambda e, f=fc, p=fp: e.activation(
                        h2_sb[p][:, :], ps_z[2][:, 0:TT], AF.Relu, bias=bi2_sb[:, f:f + 1]),
                        waits=dedup([B_z[2].ready[0]] + B_h2[fp].wr_waits()), inc=sSE)
                    B_z[1].add_reader(t_hb)
                    B_z[2].add_reader(t_hb)
                    B_h2[fp].set_ready(t_hb)
                    t = em.emit("vector", lambda e, f=fc, p=fp: e.tensor_tensor(
                        hsum[:, f * TT:(f + 1) * TT], hsum[:, f * TT:(f + 1) * TT],
                        h2_sb[p][:, :], ALU.add),
                        waits=dedup([t_hb]), inc=sVE, war=(fc != NFC - 1))
                    B_h2[fp].add_reader(t)
                    return t

                h2_toks = [None] * NFC
                for fc in range(NFC + 1):
                    if fc < NFC:
                        emit_base(fc)
                    if fc >= 1:
                        t_h2 = emit_z(fc - 1)
                        h2_toks[fc - 1] = t_h2
                B_hs.set_ready(t_h2)
                B_xf[par].add_reader((sPE, sPE.n))
                B_s1[par].add_reader((sPE, sPE.n))
                B_s2[par].add_reader((sPE, sPE.n))

                t_wo_done = None
                for tsub in range(2):
                    for dh in range(2):
                        wb = tsub * 2 + dh
                        mw = dedup([t_wo2, t_bo2] + B_w[wb].wr_waits())
                        for fc in range(NFC):
                            em.emit("tensor", lambda e, f=fc, ts=tsub, d=dh, b=wb: e.matmul(
                                ps_wo[b][:, 0:512],
                                hsum[:, f * TT + ts * 128: f * TT + ts * 128 + 128],
                                wio_sb[:, f * D + d * 512: f * D + (d + 1) * 512],
                                start=(f == 0), stop=False),
                                waits=dedup((mw if fc == 0 else []) + [h2_toks[min(fc | 3, NFC - 1)]]))
                        t_wo_done = em.emit("tensor", lambda e, d=dh, b=wb: e.matmul(
                            ps_wo[b][:, 0:512], ones_row[0:1, 0:128], bo2_sb[0:1, d * 512:(d + 1) * 512],
                            start=False, stop=True), inc=sPE)
                        B_w[wb].set_ready(t_wo_done)
                B_hs.add_reader(t_wo_done)

                t_ev = None
                for tsub in range(2):
                    col = s * 16 + (t0 // 128) + tsub
                    for dh in range(2):
                        wb = tsub * 2 + dh
                        t_ev = em.emit("vector", lambda e, ts=tsub, d=dh, b=wb, p=par, cc=col: e.scalar_tensor_tensor(
                            attnp_f[p][:, ts * D + d * 512: ts * D + (d + 1) * 512],
                            ps_wo[b][:, 0:512], vs_all[:, cc:cc + 1],
                            attnp_f[p][:, ts * D + d * 512: ts * D + (d + 1) * 512],
                            ALU.mult, ALU.add),
                            waits=dedup(B_w[wb].ready + [t_la]), inc=sVE)
                        B_w[wb].add_reader(t_ev)
                B_af[par].add_reader(t_ev)
                t_st = em.emit("scalar", lambda e, ss=s, tt=t0, p=par: e.dma_start(
                    out=out_hbm[ss * T + tt: ss * T + tt + TT, :].rearrange("(c p) d -> p c d", p=128),
                    in_=attnp_f[p][:, :].rearrange("p (c d) -> p c d", d=D)),
                    waits=[t_ev], inc=sQS)
                B_af[par].add_reader(t_st)

            final_store = (sQS, sQS.n)

        em.emit("sync", lambda e: e.nop(), waits=[final_store])

        with nc.Block() as block:
            @block.sync
            def _(sync):
                for op in em.ops["sync"]:
                    op(sync)

            @block.gpsimd
            def _(gpsimd):
                for op in em.ops["gpsimd"]:
                    op(gpsimd)

            @block.tensor
            def _(tensor):
                for op in em.ops["tensor"]:
                    op(tensor)

            @block.vector
            def _(vector):
                for op in em.ops["vector"]:
                    op(vector)

            @block.scalar
            def _(scalar):
                for op in em.ops["scalar"]:
                    op(scalar)

    return nc


_BF = ml_dtypes.bfloat16


def _consts():
    ident = np.eye(128, dtype=_BF)
    jl = np.arange(128)[:, None]
    il = np.arange(128)[None, :]
    maskA = (jl > il).astype(_BF)
    maskB = (jl <= il).astype(_BF)
    ones_row = np.ones((1, 512), dtype=_BF)
    rep = np.kron(np.eye(NE, dtype=np.float32), np.ones((1, RK), dtype=np.float32)).astype(_BF)
    return ident, maskA, maskB, ones_row, rep


def kernel(L0_attn_x, L0_attn_Wq, L0_attn_Wk, L0_attn_Wv, L0_attn_Wo,
           L0_ffn_wi, L0_ffn_bi, L0_ffn_wo, L0_ffn_bo,
           L0_ffn_lora_A, L0_ffn_lora_B, L0_ffn_gate_w, L0_ffn_gate_b):
    x = np.asarray(L0_attn_x, dtype=np.float32)
    shared = dict(
        wqT=np.ascontiguousarray(np.asarray(L0_attn_Wq, np.float32).T),
        wkT=np.ascontiguousarray(np.asarray(L0_attn_Wk, np.float32).T),
        wvT=np.ascontiguousarray(np.asarray(L0_attn_Wv, np.float32).T),
        woT=np.ascontiguousarray(np.asarray(L0_attn_Wo, np.float32).T),
        wiT=np.ascontiguousarray(np.asarray(L0_ffn_wi, np.float32).T),
        wioT=np.ascontiguousarray(np.asarray(L0_ffn_wo, np.float32).T),
        bi2=np.ascontiguousarray(np.asarray(L0_ffn_bi, np.float32).reshape(NFC, 128).T),
        bo=np.asarray(L0_ffn_bo, np.float32).reshape(1, D),
        aT=np.ascontiguousarray(np.asarray(L0_ffn_lora_A, np.float32).transpose(2, 0, 1).reshape(D, NE * RK)),
        bst=np.ascontiguousarray(np.asarray(L0_ffn_lora_B, np.float32).transpose(0, 2, 1).reshape(NE * RK, DFF)),
        gwT=np.ascontiguousarray(np.asarray(L0_ffn_gate_w, np.float32).T),
        gb=np.asarray(L0_ffn_gate_b, np.float32).reshape(1, NE),
    )
    ident, maskA, maskB, ones_row, rep = _consts()
    shared.update(ident=ident, maskA=maskA, maskB=maskB, ones_row=ones_row, rep=rep)

    nc = build_program()
    in_maps = []
    for c in range(NCORES):
        m = dict(shared)
        m["x"] = np.ascontiguousarray(x[c * SEQ:(c + 1) * SEQ].reshape(SEQ * T, D))
        in_maps.append(m)
    res = run_bass_kernel_spmd(nc, in_maps, core_ids=list(range(NCORES)))
    out = np.empty((NCORES * SEQ, T, D), dtype=np.float32)
    for c in range(NCORES):
        out[c * SEQ:(c + 1) * SEQ] = np.asarray(res.results[c]["out"]).reshape(SEQ, T, D)
    return out


